# version 18
# baseline (speedup 1.0000x reference)
import contextlib
import os
import numpy as np
import concourse.bass as bass
import concourse.mybir as mybir
from concourse.bass_utils import run_bass_kernel_spmd

F32 = mybir.dt.float32
BF16 = mybir.dt.bfloat16
FP8 = mybir.dt.float8e4
AF = mybir.ActivationFunctionType
ALU = mybir.AluOpType
AX = mybir.AxisListType

C32 = float(np.float32(1.5 * 2 ** 23))
C52 = float(np.float32(1.5 * 2 ** 52))

D = 1024
H = 16
DH = 4096
KT = 8
HT = 32
CP = 128

PENDING = -1


class Sb:

    ENGS = ("tensor", "vector", "scalar", "gpsimd", "sync")
    NDSEM = 16

    def __init__(self, nc, ctx):
        self.nc = nc
        self.eng = {"tensor": nc.tensor, "vector": nc.vector, "scalar": nc.scalar,
                    "gpsimd": nc.gpsimd, "sync": nc.sync}
        self.sem = {}
        for name in self.ENGS:
            self.sem[name] = ctx.enter_context(nc.semaphore("sb_" + name))
        for q in ("dsync", "dgp"):
            for i in range(self.NDSEM):
                nm = "%s%d" % (q, i)
                self.sem[nm] = ctx.enter_context(nc.semaphore("sb_" + nm))
        self.cnt = {name: 0 for name in self.sem}
        self.step = {name: (16 if name.startswith("d") else 1) for name in self.sem}
        self.qn = {"dsync": 0, "dgp": 0}
        self.lastw = {}
        self.readers = {}
        self.water = {}
        self.pending = {e: [] for e in self.ENGS}

    def _flush(self, src):
        self.cnt[src] += 1
        self.eng[src].nop().then_inc(self.sem[src], 1)
        self._resolve(src)

    def _resolve(self, src):
        c = self.cnt[src]
        for key, kind in self.pending[src]:
            if kind == "w":
                s, v = self.lastw.get(key, (None, None))
                if s == src and v == PENDING:
                    self.lastw[key] = (src, c)
            else:
                d = self.readers.get(key)
                if d is not None and d.get(src) == PENDING:
                    d[src] = c
        self.pending[src] = []

    def _wait(self, eng, src, cnt):
        if cnt == PENDING:
            if src == eng:
                return
            self._flush(src)
            cnt = self.cnt[src]
        if cnt <= self.water.get((eng, src), 0):
            return
        self.eng[eng].wait_ge(self.sem[src], cnt * self.step[src])
        self.water[(eng, src)] = cnt

    def _pre(self, eng, reads, writes):
        for r in reads:
            lw = self.lastw.get(id(r))
            if lw is not None:
                self._wait(eng, lw[0], lw[1])
        for w in writes:
            lw = self.lastw.get(id(w))
            if lw is not None:
                self._wait(eng, lw[0], lw[1])
            for src, c in list(self.readers.get(id(w), {}).items()):
                self._wait(eng, src, c)

    def _post(self, src, reads, writes, cnt):
        for w in writes:
            self.lastw[id(w)] = (src, cnt)
            self.readers[id(w)] = {}
            if cnt == PENDING:
                self.pending[src].append((id(w), "w"))
        for r in reads:
            self.readers.setdefault(id(r), {})[src] = cnt
            if cnt == PENDING:
                self.pending[src].append((id(r), "r"))

    budget = [10**9]

    def op(self, eng, emit, reads=(), writes=(), inc=True):
        if Sb.budget[0] <= 0:
            return None
        Sb.budget[0] -= 1
        if Sb.budget[0] == 0:
            inc = True
        self._pre(eng, reads, writes)
        inst = emit(self.eng[eng])
        if inc:
            self.cnt[eng] += 1
            inst.then_inc(self.sem[eng], 1)
            self._resolve(eng)
            c = self.cnt[eng]
        else:
            c = PENDING
        self._post(eng, reads, writes, c)
        return inst

    def dma(self, queue, out, in_, reads=(), writes=()):
        if Sb.budget[0] <= 0:
            return None
        Sb.budget[0] -= 1
        q = {"sync": "dsync", "gpsimd": "dgp"}[queue]
        n = self.qn[q]
        self.qn[q] = n + 1
        src = "%s%d" % (q, n % self.NDSEM)
        self._pre(queue, reads, writes)
        inst = self.eng[queue].dma_start(out=out, in_=in_)
        self.cnt[src] += 1
        inst.then_inc(self.sem[src], 16)
        self._post(src, reads, writes, self.cnt[src])

    def barrier(self):
        for e in self.ENGS:
            if self.pending[e]:
                self._flush(e)
        for e in self.ENGS:
            for s in self.sem:
                if s != e:
                    self._wait(e, s, self.cnt[s])


def build_program(T=2048, dbg=False, phases="WAB", a_stages=99, b_stages=99):
    nc = bass.Bass("TRN2", target_bir_lowering=False, detect_race_conditions=False)
    NCH = T // CP
    GRP = min(512, T)
    SUB = GRP // CP
    NG = T // GRP

    dx = nc.dram_tensor("x", [D, T], F32, kind="ExternalInput").ap()
    dqkvT = nc.dram_tensor("qkvT", [D, 3 * D], F32, kind="ExternalInput").ap()
    dqkvb = nc.dram_tensor("qkvb", [128, 16], F32, kind="ExternalInput").ap()
    dvbias = nc.dram_tensor("vbias_bc", [128, D], F32, kind="ExternalInput").ap()
    doutT = nc.dram_tensor("outT", [D, D], F32, kind="ExternalInput").ap()
    doutb = nc.dram_tensor("outb", [128, 8], F32, kind="ExternalInput").ap()
    dw1T = nc.dram_tensor("w1T", [D, DH], F32, kind="ExternalInput").ap()
    dw2T = nc.dram_tensor("w2T", [DH, D], F32, kind="ExternalInput").ap()
    dcw = [nc.dram_tensor("convw%d" % t, [128, HT], F32, kind="ExternalInput").ap()
           for t in range(3)]
    dcb = nc.dram_tensor("convb", [128, HT], F32, kind="ExternalInput").ap()
    dmask = nc.dram_tensor("mask4d", [128, 512], F32, kind="ExternalInput").ap()
    dident = nc.dram_tensor("identd", [128, 128], F32, kind="ExternalInput").ap()
    dones = nc.dram_tensor("onesd", [128, 128], F32, kind="ExternalInput").ap()

    dx2 = nc.dram_tensor("x2d", [D, T], F32).ap()
    dqk = nc.dram_tensor("qkd", [2 * D, T], F32).ap()
    dxnq = nc.dram_tensor("xnqd", [D, T], BF16).ap()
    dinv = nc.dram_tensor("invd", [128, T], F32).ap()
    dvt = nc.dram_tensor("vtd", [T, 1040], F32).ap()
    dw1q = nc.dram_tensor("w1qd", [D, DH], FP8).ap()
    dw2q = nc.dram_tensor("w2qd", [DH, D], FP8).ap()
    dout = nc.dram_tensor("out", [D, T], F32, kind="ExternalOutput").ap()

    ddbg = {}
    if dbg:
        for nm, shp in (("xn", [128, KT * CP]), ("qkfm", [128, 16 * CP]),
                        ("vtm", [128, H * 65]), ("kvext", [128, 520]),
                        ("aofm", [128, KT * CP]), ("po", [128, KT * CP]),
                        ("hbuf", [128, HT * (CP + 2)]), ("hs", [128, HT * CP]),
                        ("gam", [128, 4]), ("xnq", [128, KT * CP]),
                        ("x2c", [128, KT * GRP])):
            ddbg[nm] = nc.dram_tensor("dbg_" + nm, shp, F32, kind="ExternalOutput").ap()

    with contextlib.ExitStack() as ctx:
        sb = Sb(nc, ctx)

        ident = ctx.enter_context(nc.sbuf_tensor("ident", [128, 128], F32))
        ones = ctx.enter_context(nc.sbuf_tensor("ones", [128, 128], F32))
        flagg = ctx.enter_context(nc.sbuf_tensor("flagg", [128, 2], F32))
        pers = ctx.enter_context(nc.sbuf_tensor("pers", [128, 4], F32))
        pb = [ctx.enter_context(nc.psum_tensor("pb%d" % i, [128, 512], F32))
              for i in range(8)]

        ctx.enter_context(nc.Block())

        sb.dma("sync", ident[:], dident, writes=(ident,))
        sb.dma("sync", ones[:], dones, writes=(ones,))

        sb.op("vector", lambda v: v.tensor_scalar(
            out=flagg[:, 0:1], in0=ones[:, 0:1], scalar1=float(2.0 ** -28),
            scalar2=C32, op0=ALU.mult, op1=ALU.add), reads=(ones,), writes=(flagg,))
        sb.op("vector", lambda v: v.tensor_scalar(
            out=flagg[:, 0:1], in0=flagg[:, 0:1], scalar1=C32,
            scalar2=float(2.0 ** 28), op0=ALU.subtract, op1=ALU.mult),
            reads=(flagg,), writes=(flagg,))
        sb.op("vector", lambda v: v.tensor_scalar(
            out=flagg[:, 1:2], in0=flagg[:, 0:1], scalar1=-1.0, scalar2=1.0,
            op0=ALU.mult, op1=ALU.add), reads=(flagg,), writes=(flagg,))

        WCH = 2048

        def make_w_stream(A):
            wch = [A(nc.sbuf_tensor("wch%d" % i, [128, WCH], F32)) for i in range(2)]
            sAc = A(nc.sbuf_tensor("sAc", [128, WCH], F32))
            stgc = [A(nc.sbuf_tensor("stgc%d" % i, [128, WCH], FP8)) for i in range(2)]
            parts = A(nc.sbuf_tensor("wparts", [128, 36], F32))
            gscw = A(nc.sbuf_tensor("gscw", [1, 4], F32))

            def chunk_aps(dfull, n_cols, ci, sbt):
                if n_cols >= WCH:
                    spt = n_cols // WCH
                    return (dfull[:, ci // spt, WCH * (ci % spt):WCH * (ci % spt) + WCH],
                            sbt[:])
                tpc = WCH // n_cols
                return (dfull[:, tpc * ci:tpc * ci + tpc, :],
                        sbt[:].rearrange("p (t c) -> p t c", c=n_cols))

            def gen():
                for wi, (dwf, dwq, n_tiles, n_cols, pcol) in enumerate(
                        ((dw1T, dw1q, KT, DH, 0), (dw2T, dw2q, HT, D, 2))):
                    dfull = dwf.rearrange("(t p) c -> p t c", p=128)
                    dqfull = dwq.rearrange("(t p) c -> p t c", p=128)
                    nch = (n_tiles * n_cols) // WCH
                    base = 16 * wi
                    for ci in range(nch):
                        dap, sap = chunk_aps(dfull, n_cols, ci, wch[ci % 2])
                        sb.dma("sync", sap, dap, reads=(dwf,), writes=(wch[ci % 2],))
                        yield "d"
                        sb.op("vector", lambda v, ci=ci: v.reduce_sum(
                            out=parts[:, base + ci:base + ci + 1], in_=wch[ci % 2][:],
                            axis=AX.X, apply_absolute_value=True),
                            reads=(wch[ci % 2],), writes=(parts,))
                        yield "v"
                    sb.op("vector", lambda v, nch=nch, base=base: v.reduce_sum(
                        out=parts[:, 32 + wi:33 + wi], in_=parts[:, base:base + nch],
                        axis=AX.X), reads=(parts,), writes=(parts,))
                    sb.op("tensor", lambda t, wi=wi: t.matmul(
                        pb[0][0:1, 0:1], parts[:, 32 + wi:33 + wi], ones[:, 0:1],
                        start=True, stop=True), reads=(parts, ones), writes=(pb[0],))
                    sb.op("vector", lambda v, n_tiles=n_tiles, n_cols=n_cols: v.tensor_scalar(
                        out=gscw[0:1, 0:1], in0=pb[0][0:1, 0:1],
                        scalar1=1.0 / (n_tiles * 128 * n_cols), scalar2=1e-5,
                        op0=ALU.mult, op1=ALU.max), reads=(pb[0],), writes=(gscw,))
                    sb.op("vector", lambda v: v.reciprocal(
                        out=gscw[0:1, 1:2], in_=gscw[0:1, 0:1]),
                        reads=(gscw,), writes=(gscw,))
                    sb.op("vector", lambda v: v.tensor_scalar(
                        out=gscw[0:1, 2:3], in0=gscw[0:1, 0:1], scalar1=1.0 / 127.0,
                        scalar2=None, op0=ALU.mult), reads=(gscw,), writes=(gscw,))
                    for src_col, dst_col in ((1, pcol), (2, pcol + 1)):
                        sb.op("tensor", lambda t, sc=src_col: t.matmul(
                            pb[0][:, 0:1], ones[0:1, 0:128], gscw[0:1, sc:sc + 1],
                            start=True, stop=True), reads=(ones, gscw), writes=(pb[0],))
                        sb.op("vector", lambda v, dc=dst_col: v.tensor_copy(
                            out=pers[:, dc:dc + 1], in_=pb[0][:, 0:1]),
                            reads=(pb[0],), writes=(pers,))
                    yield "v"
                    for ci in range(nch):
                        dap, sap = chunk_aps(dfull, n_cols, ci, wch[ci % 2])
                        dqap, sqap = chunk_aps(dqfull, n_cols, ci, stgc[ci % 2])
                        sb.dma("sync", sap, dap, reads=(dwf,), writes=(wch[ci % 2],))
                        yield "d"
                        sb.op("vector", lambda v, ci=ci, pcol=pcol: v.tensor_scalar(
                            out=sAc[:], in0=wch[ci % 2][:], scalar1=pers[:, pcol:pcol + 1],
                            scalar2=0.5, op0=ALU.mult, op1=ALU.is_gt),
                            reads=(wch[ci % 2], pers), writes=(sAc,))
                        yield "v"
                        sb.op("vector", lambda v, ci=ci, pcol=pcol: v.tensor_scalar(
                            out=wch[ci % 2][:], in0=wch[ci % 2][:],
                            scalar1=pers[:, pcol:pcol + 1],
                            scalar2=-0.5, op0=ALU.mult, op1=ALU.is_lt),
                            reads=(wch[ci % 2], pers), writes=(wch[ci % 2],))
                        yield "v"
                        sb.op("vector", lambda v, ci=ci: v.tensor_tensor(
                            out=stgc[ci % 2][:], in0=sAc[:], in1=wch[ci % 2][:],
                            op=ALU.subtract),
                            reads=(sAc, wch[ci % 2]), writes=(stgc[ci % 2],))
                        yield "v"
                        sb.dma("gpsimd", dqap, sqap,
                               reads=(stgc[ci % 2],), writes=(dwq,))
                        yield "d"
            return gen()

        def make_ln(xsq_t, stt_t, abtm_t, abrow_t, bc_t):
            def emit_ln(src):
                sb.op("scalar", lambda s: s.activation(
                    out=xsq_t[:], in_=src[:], func=AF.Square),
                    reads=(src,), writes=(xsq_t,))
                for col, ten in ((0, src), (1, xsq_t)):
                    for k in range(KT):
                        sb.op("tensor", lambda t, k=k, col=col, ten=ten: t.matmul(
                            pb[4][:, col:col + 1], ten[:, k * CP:(k + 1) * CP],
                            ones[:, 0:1], start=(k == 0), stop=(k == KT - 1)),
                            reads=(ten, ones), writes=(pb[4],), inc=(k == KT - 1))
                sb.op("vector", lambda v: v.tensor_scalar(
                    out=stt_t[:, 0:2], in0=pb[4][:, 0:2], scalar1=1.0 / D,
                    scalar2=None, op0=ALU.mult), reads=(pb[4],), writes=(stt_t,))
                sb.op("vector", lambda v: v.tensor_tensor(
                    out=stt_t[:, 2:3], in0=stt_t[:, 0:1], in1=stt_t[:, 0:1],
                    op=ALU.mult), reads=(stt_t,), writes=(stt_t,))
                sb.op("vector", lambda v: v.tensor_scalar(
                    out=stt_t[:, 2:3], in0=stt_t[:, 2:3], scalar1=-1.0, scalar2=1e-5,
                    op0=ALU.mult, op1=ALU.add), reads=(stt_t,), writes=(stt_t,))
                sb.op("vector", lambda v: v.tensor_tensor(
                    out=stt_t[:, 3:4], in0=stt_t[:, 1:2], in1=stt_t[:, 2:3],
                    op=ALU.add), reads=(stt_t,), writes=(stt_t,))
                sb.op("scalar", lambda s: s.activation(
                    out=stt_t[:, 4:5], in_=stt_t[:, 3:4], func=AF.Sqrt),
                    reads=(stt_t,), writes=(stt_t,))
                sb.op("vector", lambda v: v.reciprocal(
                    out=abtm_t[:, 0:1], in_=stt_t[:, 4:5]),
                    reads=(stt_t,), writes=(abtm_t,))
                sb.op("vector", lambda v: v.tensor_scalar(
                    out=stt_t[:, 5:6], in0=stt_t[:, 0:1], scalar1=-1.0, scalar2=None,
                    op0=ALU.mult), reads=(stt_t,), writes=(stt_t,))
                sb.op("vector", lambda v: v.tensor_tensor(
                    out=abtm_t[:, 1:2], in0=stt_t[:, 5:6], in1=abtm_t[:, 0:1],
                    op=ALU.mult), reads=(stt_t, abtm_t), writes=(abtm_t,))
                for i in range(2):
                    sb.op("tensor", lambda t, i=i: t.transpose(
                        pb[4][0:1, 16 + 128 * i:144 + 128 * i], abtm_t[:, i:i + 1],
                        ident[:]), reads=(abtm_t, ident), writes=(pb[4],),
                        inc=(i == 1))
                sb.op("vector", lambda v: v.tensor_copy(
                    out=abrow_t[0:1, 0:256], in_=pb[4][0:1, 16:272]),
                    reads=(pb[4],), writes=(abrow_t,))
                for i in range(2):
                    sb.op("tensor", lambda t, i=i: t.matmul(
                        pb[4][:, 256 + 128 * i:384 + 128 * i], ones[0:1, 0:128],
                        abrow_t[0:1, 128 * i:128 * (i + 1)], start=True, stop=True),
                        reads=(ones, abrow_t), writes=(pb[4],))
                sb.op("vector", lambda v: v.tensor_copy(
                    out=bc_t[:, 0:256], in_=pb[4][:, 256:512]),
                    reads=(pb[4],), writes=(bc_t,))
            return emit_ln

        def emit_ln_group(srct, pbt0, pbt1, pbb0, pbb1, bst, mv, abt, abrow_t, bc_t,
                          n_sub=None):
            if n_sub is None:
                n_sub = SUB
            for t_ in range(n_sub):
                for k in range(KT):
                    bank = pbt0 if k < 4 else pbt1
                    sb.op("tensor", lambda t, t_=t_, k=k, bank=bank: t.transpose(
                        bank[:, 128 * (k % 4):128 * (k % 4 + 1)],
                        srct[:, GRP * k + 128 * t_:GRP * k + 128 * (t_ + 1)],
                        ident[:]), reads=(srct, ident), writes=(bank,),
                        inc=(k % 4 == 3))
                sb.op("vector", lambda v, t_=t_: v.bn_stats(
                    out=bst[:, 12 * t_:12 * t_ + 6], in_=pbt0[:, 0:512]),
                    reads=(pbt0,), writes=(bst,))
                sb.op("vector", lambda v, t_=t_: v.bn_stats(
                    out=bst[:, 12 * t_ + 6:12 * t_ + 12], in_=pbt1[:, 0:512]),
                    reads=(pbt1,), writes=(bst,))
                sb.op("vector", lambda v, t_=t_: v.bn_aggr(
                    out=mv[:, 2 * t_:2 * t_ + 2], in_=bst[:, 12 * t_:12 * t_ + 12]),
                    reads=(bst,), writes=(mv,))
            mve = mv[:].rearrange("p (t c) -> p t c", c=2)
            abte = abt[:].rearrange("p (a b) -> p a b", b=1)
            sb.op("vector", lambda v, n_sub=n_sub: v.tensor_scalar(
                out=abte[:, 8:8 + n_sub, :], in0=mve[:, 0:n_sub, 1:2], scalar1=1e-5,
                scalar2=None, op0=ALU.add), reads=(mv,), writes=(abt,))
            sb.op("scalar", lambda s, n_sub=n_sub: s.activation(
                out=abt[:, 8:8 + n_sub], in_=abt[:, 8:8 + n_sub], func=AF.Sqrt),
                reads=(abt,), writes=(abt,))
            sb.op("vector", lambda v, n_sub=n_sub: v.reciprocal(
                out=abt[:, 0:n_sub], in_=abt[:, 8:8 + n_sub]), reads=(abt,), writes=(abt,))
            sb.op("vector", lambda v, n_sub=n_sub: v.tensor_scalar(
                out=abte[:, 12:12 + n_sub, :], in0=mve[:, 0:n_sub, 0:1], scalar1=-1.0,
                scalar2=None, op0=ALU.mult), reads=(mv,), writes=(abt,))
            sb.op("vector", lambda v, n_sub=n_sub: v.tensor_tensor(
                out=abt[:, 4:4 + n_sub], in0=abt[:, 12:12 + n_sub], in1=abt[:, 0:n_sub],
                op=ALU.mult), reads=(abt,), writes=(abt,))
            idxs = list(range(n_sub)) + [4 + i for i in range(n_sub)]
            for j, i in enumerate(idxs):
                bank = pbt0 if j < 4 else pbt1
                sb.op("tensor", lambda t, j=j, i=i, bank=bank: t.transpose(
                    bank[0:1, 128 * (j % 4):128 * (j % 4) + 128], abt[:, i:i + 1],
                    ident[:]), reads=(abt, ident), writes=(bank,),
                    inc=(j == len(idxs) - 1))
            sb.op("vector", lambda v, n_sub=n_sub: v.tensor_copy(
                out=abrow_t[0:1, 0:min(512, 256 * n_sub)],
                in_=pbt0[0:1, 0:min(512, 256 * n_sub)]),
                reads=(pbt0,), writes=(abrow_t,))
            if 2 * n_sub > 4:
                sb.op("vector", lambda v, n_sub=n_sub: v.tensor_copy(
                    out=abrow_t[0:1, 512:256 * n_sub + 256 * n_sub - 512 + 512]
                    if False else abrow_t[0:1, 512:128 * 2 * n_sub],
                    in_=pbt1[0:1, 0:128 * 2 * n_sub - 512]),
                    reads=(pbt1,), writes=(abrow_t,))
            for j in range(2 * n_sub):
                bank = pbb0 if j < n_sub else pbb1
                sb.op("tensor", lambda t, j=j, bank=bank, n_sub=n_sub: t.matmul(
                    bank[:, 128 * (j % n_sub):128 * (j % n_sub + 1)], ones[0:1, 0:128],
                    abrow_t[0:1, 128 * j:128 * (j + 1)], start=True, stop=True),
                    reads=(ones, abrow_t), writes=(bank,),
                    inc=(j % n_sub == n_sub - 1))
            sb.op("vector", lambda v: v.tensor_copy(
                out=bc_t[:, 0:GRP], in_=pbb0[:, 0:GRP]), reads=(pbb0,), writes=(bc_t,))
            sb.op("vector", lambda v: v.tensor_copy(
                out=bc_t[:, GRP:2 * GRP], in_=pbb1[:, 0:GRP]),
                reads=(pbb1,), writes=(bc_t,))

        with contextlib.ExitStack() as actx:
            A = actx.enter_context
            qkvT_s = A(nc.sbuf_tensor("qkvT_s", [128, KT * 3 * D], F32))
            qkvb_s = A(nc.sbuf_tensor("qkvb_s", [128, 16], F32))
            vbias_s = A(nc.sbuf_tensor("vbias_s", [128, D], F32))
            xg = [A(nc.sbuf_tensor("xg%d" % i, [128, KT * GRP], F32)) for i in range(2)]
            xng = A(nc.sbuf_tensor("xng", [128, KT * GRP], F32))
            qkg = A(nc.sbuf_tensor("qkg", [128, 16 * GRP], F32))
            vtg = A(nc.sbuf_tensor("vtg", [128, SUB * 1040], F32))
            bstA = A(nc.sbuf_tensor("bstA", [128, 48], F32))
            mvA = A(nc.sbuf_tensor("mvA", [128, 8], F32))
            abtA = A(nc.sbuf_tensor("abtA", [128, 16], F32))
            abrowA = A(nc.sbuf_tensor("abrowA", [1, 1024], F32))
            bcA = A(nc.sbuf_tensor("bcA", [128, 1024], F32))

            sb.dma("sync", qkvT_s[:].rearrange("p (t c) -> p t c", c=3 * D),
                   dqkvT.rearrange("(t p) c -> p t c", p=128), writes=(qkvT_s,))
            sb.dma("sync", qkvb_s[:], dqkvb, writes=(qkvb_s,))
            sb.dma("sync", vbias_s[:], dvbias, writes=(vbias_s,))
            sb.op("vector", lambda v: v.memset(
                vtg[:].rearrange("p (h c) -> p h c", c=65)[:, :, 64:65], 1.0),
                writes=(vtg,))

            def load_xg(gi):
                sb.dma("sync", xg[gi % 2][:].rearrange("p (t n) -> p t n", n=GRP),
                       dx.rearrange("(t p) n -> p t n", p=128)[:, :, gi * GRP:(gi + 1) * GRP],
                       writes=(xg[gi % 2],))

            if 'A' in phases:
                load_xg(0)
            for g in range(NG if 'A' in phases else 0):
                xgc = xg[g % 2]
                if g + 1 < NG:
                    load_xg(g + 1)
                emit_ln_group(xgc, pb[4], pb[5], pb[6], pb[7], bstA, mvA,
                              abtA, abrowA, bcA)
                for k in range(KT):
                    sb.op("vector", lambda v, k=k, xgc=xgc: v.tensor_tensor(
                        out=xng[:, k * GRP:(k + 1) * GRP],
                        in0=xgc[:, k * GRP:(k + 1) * GRP],
                        in1=bcA[:, 0:GRP], op=ALU.mult),
                        reads=(xgc, bcA), writes=(xng,))
                    sb.op("vector", lambda v, k=k: v.tensor_tensor(
                        out=xng[:, k * GRP:(k + 1) * GRP],
                        in0=xng[:, k * GRP:(k + 1) * GRP],
                        in1=bcA[:, GRP:2 * GRP], op=ALU.add),
                        reads=(xng, bcA), writes=(xng,))
                for ot in range(16):
                    bank = pb[ot % 2]
                    for k in range(KT):
                        sb.op("tensor", lambda t, ot=ot, k=k, bank=bank: t.matmul(
                            bank[:, 0:GRP],
                            qkvT_s[:, (3 * D) * k + 128 * ot:(3 * D) * k + 128 * (ot + 1)],
                            xng[:, k * GRP:(k + 1) * GRP],
                            start=(k == 0), stop=(k == KT - 1)),
                            reads=(qkvT_s, xng), writes=(bank,), inc=(k == KT - 1))
                    sb.op("scalar", lambda s, ot=ot, bank=bank: s.activation(
                        out=qkg[:, ot * GRP:(ot + 1) * GRP], in_=bank[:, 0:GRP],
                        func=AF.Relu, bias=qkvb_s[:, ot:ot + 1], scale=1.0),
                        reads=(bank, qkvb_s), writes=(qkg,))
                sb.op("vector", lambda v: v.tensor_scalar(
                    out=qkg[:], in0=qkg[:], scalar1=1e-6, scalar2=None, op0=ALU.add),
                    reads=(qkg,), writes=(qkg,))
                sb.dma("gpsimd",
                       dqk.rearrange("(t p) n -> p t n", p=128)[:, :, g * GRP:(g + 1) * GRP],
                       qkg[:].rearrange("p (t n) -> p t n", n=GRP),
                       reads=(qkg,), writes=(dqk,))
                for t_ in range(SUB):
                    for half in range(2):
                        bank = pb[2 + half]
                        for k in range(KT):
                            sb.op("tensor", lambda t, t_=t_, half=half, k=k, bank=bank: t.matmul(
                                bank[:, 0:512],
                                xng[:, GRP * k + 128 * t_:GRP * k + 128 * (t_ + 1)],
                                qkvT_s[:, (3 * D) * k + 2 * D + 512 * half:
                                       (3 * D) * k + 2 * D + 512 * (half + 1)],
                                start=(k == 0), stop=(k == KT - 1)),
                                reads=(xng, qkvT_s), writes=(bank,), inc=(k == KT - 1))
                        sb.op("vector", lambda v, t_=t_, half=half, bank=bank: v.tensor_tensor(
                            out=vtg[:].rearrange("p (h c) -> p h c", c=65)[
                                :, 16 * t_ + 8 * half:16 * t_ + 8 * half + 8, 0:64],
                            in0=bank[:, 0:512].rearrange("p (h c) -> p h c", c=64),
                            in1=vbias_s[:, 512 * half:512 * (half + 1)].rearrange(
                                "p (h c) -> p h c", c=64),
                            op=ALU.add), reads=(bank, vbias_s), writes=(vtg,))
                sb.dma("gpsimd",
                       dvt.rearrange("(c p) n -> p c n", p=128)[:, SUB * g:SUB * g + SUB, :],
                       vtg[:].rearrange("p (c n) -> p c n", n=1040),
                       reads=(vtg,), writes=(dvt,))
            sb.barrier()

        with contextlib.ExitStack() as actx:
            A = actx.enter_context
            outT_s = A(nc.sbuf_tensor("outT_s", [128, KT * D], F32))
            outb_s = A(nc.sbuf_tensor("outb_s", [128, 8], F32))
            mask4 = A(nc.sbuf_tensor("mask4", [128, 512], F32))
            qkc = [A(nc.sbuf_tensor("qkc%d" % i, [128, 16 * CP], F32)) for i in range(2)]
            vtc = [A(nc.sbuf_tensor("vtc%d" % i, [128, 1040], F32)) for i in range(2)]
            xgr = [A(nc.sbuf_tensor("xgr%d" % i, [128, KT * GRP], F32)) for i in range(2)]
            ktm = A(nc.sbuf_tensor("ktm", [128, D], F32))
            smb = A(nc.sbuf_tensor("smb", [128, 512], F32))
            kvext = A(nc.sbuf_tensor("kvext", [128, 520], F32))
            aotm = A(nc.sbuf_tensor("aotm", [128, KT * CP], F32))
            ao4 = A(nc.sbuf_tensor("ao4", [128, KT * GRP], F32))
            po4 = A(nc.sbuf_tensor("po4", [128, KT * GRP], F32))
            den4 = A(nc.sbuf_tensor("den4", [128, 16], F32))
            bstA = A(nc.sbuf_tensor("bstA1", [128, 48], F32))
            mvA = A(nc.sbuf_tensor("mvA1", [128, 8], F32))
            abtA = A(nc.sbuf_tensor("abtA1", [128, 16], F32))
            abrowA = A(nc.sbuf_tensor("abrowA1", [1, 1024], F32))
            bcA = A(nc.sbuf_tensor("bcA1", [128, 1024], F32))
            bcQ4 = A(nc.sbuf_tensor("bcQ4", [128, 1024], F32))
            xn3g = A(nc.sbuf_tensor("xn3g", [128, KT * GRP], F32))
            xnqg = A(nc.sbuf_tensor("xnqg", [128, KT * GRP], BF16))

            sb.dma("sync", outT_s[:].rearrange("p (t c) -> p t c", c=D),
                   doutT.rearrange("(t p) c -> p t c", p=128), writes=(outT_s,))
            sb.dma("sync", outb_s[:], doutb, writes=(outb_s,))
            sb.dma("sync", mask4[:], dmask, writes=(mask4,))
            sb.op("vector", lambda v: v.memset(kvext[:], 0.0), writes=(kvext,))

            def load_chunk(ci):
                sb.dma("sync", qkc[ci % 2][:].rearrange("p (t n) -> p t n", n=CP),
                       dqk.rearrange("(t p) n -> p t n", p=128)[:, :, ci * CP:(ci + 1) * CP],
                       reads=(dqk,), writes=(qkc[ci % 2],))
                sb.dma("sync", vtc[ci % 2][:],
                       dvt.rearrange("(c p) n -> p c n", p=128)[:, ci, :],
                       reads=(dvt,), writes=(vtc[ci % 2],))

            def load_xgr(gi):
                sb.dma("sync", xgr[gi % 2][:].rearrange("p (t n) -> p t n", n=GRP),
                       dx.rearrange("(t p) n -> p t n", p=128)[:, :, gi * GRP:(gi + 1) * GRP],
                       writes=(xgr[gi % 2],))

            wgen = make_w_stream(A) if "W" in phases else None
            wdriven = [0]

            def wdrive(n):
                if wgen is not None:
                    for _ in range(n):
                        if next(wgen, StopIteration) is StopIteration:
                            break
                        wdriven[0] += 1

            def wpump(vb):
                if wgen is not None:
                    while vb > 0:
                        tag = next(wgen, None)
                        if tag is None:
                            return
                        wdriven[0] += 1
                        if tag == "v":
                            vb -= 1

            def wdrive_to(k):
                wdrive(max(0, k - wdriven[0]))

            if 'A' in phases:
                load_chunk(0)
                load_xgr(0)
            for g in range(NG if 'A' in phases else 0):
                xgc = xgr[g % 2]
                if g + 1 < NG:
                    load_xgr(g + 1)
                for cc in range(SUB):
                    ci = SUB * g + cc
                    qkfm = qkc[ci % 2]
                    vtm = vtc[ci % 2]
                    if ci + 1 < NCH:
                        load_chunk(ci + 1)
                    wpump(1)
                    for t8 in range(KT):
                        bank = pb[2 + t8 // 4]
                        sb.op("tensor", lambda t, t8=t8, bank=bank, qkfm=qkfm: t.transpose(
                            bank[:, 128 * (t8 % 4):128 * (t8 % 4 + 1)],
                            qkfm[:, (8 + t8) * CP:(9 + t8) * CP], ident[:]),
                            reads=(qkfm, ident), writes=(bank,), inc=(t8 % 4 == 3))
                    for gi in range(2):
                        sb.op("scalar", lambda s, gi=gi: s.copy(
                            out=ktm[:, 512 * gi:512 * (gi + 1)], in_=pb[2 + gi][:, 0:512]),
                            reads=(pb[2 + gi],), writes=(ktm,))
                    for g4 in range(4):
                        bkA = pb[5] if g4 % 2 == 0 else pb[2]
                        bkB = pb[6] if g4 % 2 == 0 else pb[3]
                        numb = pb[6] if g4 % 2 == 0 else pb[4]
                        for hh in range(4):
                            h = 4 * g4 + hh
                            off = 64 * (h % 2)
                            qt = h // 2
                            bank = bkA if h % 2 == 0 else bkB
                            sl = 128 * (hh // 2)
                            sb.op("tensor", lambda t, off=off, qt=qt, bank=bank, sl=sl, qkfm=qkfm: t.matmul(
                                bank[:, sl:sl + 128],
                                qkfm[off:off + 64, (8 + qt) * CP:(9 + qt) * CP],
                                qkfm[off:off + 64, qt * CP:(qt + 1) * CP],
                                start=True, stop=True),
                                reads=(qkfm,), writes=(bank,), inc=(hh == 3))
                        sb.op("vector", lambda v, bkA=bkA: v.tensor_tensor(
                            out=smb[:, 0:256], in0=bkA[:, 0:256], in1=mask4[:, 0:256],
                            op=ALU.mult), reads=(bkA, mask4), writes=(smb,))
                        sb.op("vector", lambda v, bkB=bkB: v.tensor_tensor(
                            out=smb[:, 256:512], in0=bkB[:, 0:256], in1=mask4[:, 0:256],
                            op=ALU.mult), reads=(bkB, mask4), writes=(smb,))
                        for hh in range(4):
                            h = 4 * g4 + hh
                            off = 64 * (h % 2)
                            gp = h // 2
                            if gp < 7:
                                outap = pb[7][off:off + 64, 65 * gp:65 * (gp + 1)]
                                wbank = pb[7]
                            else:
                                outap = pb[5][off:off + 64, 384:449]
                                wbank = pb[5]
                            sb.op("tensor", lambda t, h=h, outap=outap, vtm=vtm: t.matmul(
                                outap, ktm[:, 64 * h:64 * (h + 1)],
                                vtm[:, 65 * h:65 * (h + 1)],
                                start=True, stop=True),
                                reads=(ktm, vtm), writes=(wbank,), inc=(hh == 3))
                        for hh in range(4):
                            h = 4 * g4 + hh
                            off = 64 * (h % 2)
                            gp = h // 2
                            ssl = 256 * (h % 2) + 128 * (hh // 2)
                            sb.op("tensor", lambda t, hh=hh, h=h, ssl=ssl, vtm=vtm, numb=numb: t.matmul(
                                numb[:, 128 * hh:128 * hh + 65],
                                smb[:, ssl:ssl + 128],
                                vtm[:, 65 * h:65 * (h + 1)], start=True, stop=False),
                                reads=(smb, vtm), writes=(numb,), inc=False)
                            sb.op("tensor", lambda t, hh=hh, h=h, off=off, gp=gp, qkfm=qkfm, numb=numb: t.matmul(
                                numb[:, 128 * hh:128 * hh + 65],
                                qkfm[off:off + 64, (h // 2) * CP:(h // 2 + 1) * CP],
                                kvext[off:off + 64, 65 * gp:65 * (gp + 1)],
                                start=False, stop=True),
                                reads=(qkfm, kvext), writes=(numb,), inc=(hh == 3))
                        sb.op("vector", lambda v, g4=g4, numb=numb: v.tensor_scalar(
                            out=den4[:, 4 * g4:4 * g4 + 4],
                            in0=numb[:, 0:512].rearrange("p (s c) -> p s c", c=128)[:, :, 64:65],
                            scalar1=1e-6, scalar2=None, op0=ALU.max),
                            reads=(numb,), writes=(den4,))
                        sb.op("vector", lambda v, g4=g4: v.reciprocal(
                            out=den4[:, 4 * g4:4 * g4 + 4], in_=den4[:, 4 * g4:4 * g4 + 4]),
                            reads=(den4,), writes=(den4,))
                        for hh in range(4):
                            h = 4 * g4 + hh
                            dst = 128 * (h // 2) + 64 * (h % 2)
                            sb.op("scalar", lambda s, hh=hh, dst=dst, g4=g4, numb=numb: s.activation(
                                out=aotm[:, dst:dst + 64],
                                in_=numb[:, 128 * hh:128 * hh + 64],
                                func=AF.Identity,
                                scale=den4[:, 4 * g4 + hh:4 * g4 + hh + 1]),
                                reads=(numb, den4), writes=(aotm,))
                        wpump(1)
                    sb.op("vector", lambda v: v.tensor_tensor(
                        out=kvext[:, 0:455], in0=kvext[:, 0:455], in1=pb[7][:, 0:455],
                        op=ALU.add), reads=(kvext, pb[7]), writes=(kvext,))
                    sb.op("vector", lambda v: v.tensor_tensor(
                        out=kvext[:, 455:520], in0=kvext[:, 455:520],
                        in1=pb[5][:, 384:449],
                        op=ALU.add), reads=(kvext, pb[5]), writes=(kvext,))
                    for pr in range(8):
                        bank = pb[2 + pr // 4]
                        sb.op("tensor", lambda t, pr=pr, bank=bank: t.transpose(
                            bank[:, 128 * (pr % 4):128 * (pr % 4 + 1)],
                            aotm[:, 128 * pr:128 * (pr + 1)], ident[:]),
                            reads=(aotm, ident), writes=(bank,), inc=(pr % 4 == 3))
                    for gi in range(2):
                        sb.op("scalar", lambda s, gi=gi, cc=cc: s.copy(
                            out=ao4[:].rearrange("p (t n) -> p t n", n=GRP)[
                                :, 4 * gi:4 * gi + 4, 128 * cc:128 * (cc + 1)],
                            in_=pb[2 + gi][:, 0:512].rearrange("p (t n) -> p t n", n=128)),
                            reads=(pb[2 + gi],), writes=(ao4,))
                for ot in range(8):
                    bank = pb[ot % 2]
                    for k in range(KT):
                        sb.op("tensor", lambda t, ot=ot, k=k, bank=bank: t.matmul(
                            bank[:, 0:GRP],
                            outT_s[:, D * k + 128 * ot:D * k + 128 * (ot + 1)],
                            ao4[:, k * GRP:(k + 1) * GRP],
                            start=(k == 0), stop=(k == KT - 1)),
                            reads=(outT_s, ao4), writes=(bank,), inc=(k == KT - 1))
                    sb.op("scalar", lambda s, ot=ot, bank=bank: s.activation(
                        out=po4[:, ot * GRP:(ot + 1) * GRP], in_=bank[:, 0:GRP],
                        func=AF.Identity, bias=outb_s[:, ot:ot + 1], scale=1.0),
                        reads=(bank, outb_s), writes=(po4,))
                wpump(10)
                emit_ln_group(po4, pb[4], pb[5], pb[6], pb[7], bstA, mvA,
                              abtA, abrowA, bcA)
                for k in range(KT):
                    sb.op("vector", lambda v, k=k: v.tensor_tensor(
                        out=po4[:, k * GRP:(k + 1) * GRP],
                        in0=po4[:, k * GRP:(k + 1) * GRP],
                        in1=bcA[:, 0:GRP], op=ALU.mult),
                        reads=(po4, bcA), writes=(po4,))
                    sb.op("vector", lambda v, k=k: v.tensor_tensor(
                        out=po4[:, k * GRP:(k + 1) * GRP],
                        in0=po4[:, k * GRP:(k + 1) * GRP],
                        in1=bcA[:, GRP:2 * GRP], op=ALU.add),
                        reads=(po4, bcA), writes=(po4,))
                    sb.op("vector", lambda v, k=k, xgc=xgc: v.tensor_tensor(
                        out=po4[:, k * GRP:(k + 1) * GRP],
                        in0=po4[:, k * GRP:(k + 1) * GRP],
                        in1=xgc[:, k * GRP:(k + 1) * GRP], op=ALU.add),
                        reads=(po4, xgc), writes=(po4,))
                sb.dma("gpsimd",
                       dx2.rearrange("(t p) n -> p t n", p=128)[:, :, g * GRP:(g + 1) * GRP],
                       po4[:].rearrange("p (t n) -> p t n", n=GRP),
                       reads=(po4,), writes=(dx2,))
                wdrive_to(33)
                emit_ln_group(po4, pb[4], pb[5], pb[6], pb[7], bstA, mvA,
                              abtA, abrowA, bcA)
                for k in range(KT):
                    sb.op("vector", lambda v, k=k: v.tensor_tensor(
                        out=xn3g[:, k * GRP:(k + 1) * GRP],
                        in0=po4[:, k * GRP:(k + 1) * GRP],
                        in1=bcA[:, 0:GRP], op=ALU.mult),
                        reads=(po4, bcA), writes=(xn3g,))
                    sb.op("vector", lambda v, k=k: v.tensor_tensor(
                        out=xn3g[:, k * GRP:(k + 1) * GRP],
                        in0=xn3g[:, k * GRP:(k + 1) * GRP],
                        in1=bcA[:, GRP:2 * GRP], op=ALU.add),
                        reads=(xn3g, bcA), writes=(xn3g,))
                sb.op("scalar", lambda s: s.activation(
                    out=ao4[:], in_=xn3g[:], func=AF.Abs), reads=(xn3g,), writes=(ao4,))
                for lv in (4, 2, 1):
                    sb.op("vector", lambda v, lv=lv: v.tensor_tensor(
                        out=ao4[:, 0:lv * GRP], in0=ao4[:, 0:lv * GRP],
                        in1=ao4[:, lv * GRP:2 * lv * GRP], op=ALU.max),
                        reads=(ao4,), writes=(ao4,))
                for t_ in range(SUB):
                    sb.op("tensor", lambda t, t_=t_: t.transpose(
                        pb[4][:, 128 * t_:128 * (t_ + 1)],
                        ao4[:, 128 * t_:128 * (t_ + 1)], ident[:]),
                        reads=(ao4, ident), writes=(pb[4],), inc=(t_ == SUB - 1))
                for t_ in range(SUB):
                    sb.op("vector", lambda v, t_=t_: v.tensor_reduce(
                        out=abtA[:].rearrange("p (a b) -> p a b", b=1)[:, 8 + t_, :],
                        in_=pb[4][:, 128 * t_:128 * (t_ + 1)], axis=AX.X, op=ALU.max),
                        reads=(pb[4],), writes=(abtA,))
                sb.op("vector", lambda v: v.tensor_scalar(
                    out=abtA[:, 8:8 + SUB], in0=abtA[:, 8:8 + SUB], scalar1=1e-5,
                    scalar2=None, op0=ALU.max), reads=(abtA,), writes=(abtA,))
                sb.op("vector", lambda v: v.reciprocal(
                    out=abtA[:, 0:SUB], in_=abtA[:, 8:8 + SUB]),
                    reads=(abtA,), writes=(abtA,))
                sb.op("vector", lambda v: v.tensor_scalar(
                    out=abtA[:, 0:SUB], in0=abtA[:, 0:SUB], scalar1=127.0,
                    scalar2=None, op0=ALU.mult), reads=(abtA,), writes=(abtA,))
                sb.op("vector", lambda v: v.tensor_scalar(
                    out=abtA[:, 4:4 + SUB], in0=abtA[:, 8:8 + SUB],
                    scalar1=pers[:, 1:2], scalar2=None, op0=ALU.mult),
                    reads=(abtA, pers), writes=(abtA,))
                idxs = list(range(SUB)) + [4 + i for i in range(SUB)]
                for j, i in enumerate(idxs):
                    bank = pb[4] if j < 4 else pb[5]
                    sb.op("tensor", lambda t, j=j, i=i, bank=bank: t.transpose(
                        bank[0:1, 128 * (j % 4):128 * (j % 4) + 128], abtA[:, i:i + 1],
                        ident[:]), reads=(abtA, ident), writes=(bank,),
                        inc=(j == len(idxs) - 1))
                sb.op("vector", lambda v: v.tensor_copy(
                    out=abrowA[0:1, 0:min(512, 256 * SUB)],
                    in_=pb[4][0:1, 0:min(512, 256 * SUB)]),
                    reads=(pb[4],), writes=(abrowA,))
                if 2 * SUB > 4:
                    sb.op("vector", lambda v: v.tensor_copy(
                        out=abrowA[0:1, 512:128 * 2 * SUB],
                        in_=pb[5][0:1, 0:128 * 2 * SUB - 512]),
                        reads=(pb[5],), writes=(abrowA,))
                for j in range(2 * SUB):
                    bank = pb[6] if j < SUB else pb[7]
                    sb.op("tensor", lambda t, j=j, bank=bank: t.matmul(
                        bank[:, 128 * (j % SUB):128 * (j % SUB + 1)], ones[0:1, 0:128],
                        abrowA[0:1, 128 * j:128 * (j + 1)], start=True, stop=True),
                        reads=(ones, abrowA), writes=(bank,), inc=(j % SUB == SUB - 1))
                sb.op("vector", lambda v: v.tensor_copy(
                    out=bcQ4[:, 0:GRP], in_=pb[6][:, 0:GRP]),
                    reads=(pb[6],), writes=(bcQ4,))
                sb.op("vector", lambda v: v.tensor_copy(
                    out=bcQ4[:, GRP:2 * GRP], in_=pb[7][:, 0:GRP]),
                    reads=(pb[7],), writes=(bcQ4,))
                for k in range(KT):
                    sb.op("vector", lambda v, k=k: v.tensor_tensor(
                        out=xn3g[:, k * GRP:(k + 1) * GRP],
                        in0=xn3g[:, k * GRP:(k + 1) * GRP],
                        in1=bcQ4[:, 0:GRP], op=ALU.mult),
                        reads=(xn3g, bcQ4), writes=(xn3g,))
                sb.op("vector", lambda v: v.tensor_scalar(
                    out=ao4[:], in0=xn3g[:], scalar1=C32, scalar2=C32,
                    op0=ALU.add, op1=ALU.subtract), reads=(xn3g,), writes=(ao4,))
                sb.op("vector", lambda v: v.tensor_scalar(
                    out=xn3g[:], in0=xn3g[:], scalar1=C52, scalar2=C52,
                    op0=ALU.add, op1=ALU.subtract), reads=(xn3g,), writes=(xn3g,))
                sb.op("vector", lambda v: v.scalar_tensor_tensor(
                    out=xnqg[:], in0=ao4[:], scalar=flagg[:, 1:2], in1=xn3g[:],
                    op0=ALU.mult, op1=ALU.add),
                    reads=(ao4, xn3g, flagg), writes=(xnqg,))
                sb.dma("gpsimd",
                       dxnq.rearrange("(t p) n -> p t n", p=128)[:, :, g * GRP:(g + 1) * GRP],
                       xnqg[:].rearrange("p (t n) -> p t n", n=GRP),
                       reads=(xnqg,), writes=(dxnq,))
                sb.dma("gpsimd", dinv[:, g * GRP:(g + 1) * GRP],
                       bcQ4[:, GRP:2 * GRP], reads=(bcQ4,), writes=(dinv,))
                wpump(2)
                if dbg and g == NG - 1:
                    for nm, ten in (("qkfm", qkc[(SUB * g + SUB - 1) % 2]),
                                    ("vtm", vtc[(SUB * g + SUB - 1) % 2]),
                                    ("kvext", kvext), ("x2c", po4)):
                        sb.dma("gpsimd", ddbg[nm], ten[:], reads=(ten,))
            wdrive(10 ** 6)
            sb.barrier()

        with contextlib.ExitStack() as bctx:
            B_ = bctx.enter_context

            w1q_s = B_(nc.sbuf_tensor("w1q_s", [128, KT * DH], FP8))
            w2q_s = B_(nc.sbuf_tensor("w2q_s", [128, HT * D], FP8))
            cw_s = [B_(nc.sbuf_tensor("cw%d_s" % t, [128, HT], F32)) for t in range(3)]
            cb_s = B_(nc.sbuf_tensor("cb_s", [128, HT], F32))
            x2b = [B_(nc.sbuf_tensor("x2b%d" % i, [128, KT * CP], F32)) for i in range(4)]
            fv = B_(nc.sbuf_tensor("fv", [128, KT * CP], F32))
            xnqc = [B_(nc.sbuf_tensor("xnqc%d" % i, [128, KT * CP], BF16)) for i in range(2)]
            invc4 = [B_(nc.sbuf_tensor("invc4%d" % i, [128, 512], F32)) for i in range(2)]
            hbuf = [B_(nc.sbuf_tensor("hbuf%d" % i, [128, HT * (CP + 2)], F32))
                    for i in range(3)]
            hs = B_(nc.sbuf_tensor("hs", [128, HT * CP], F32))
            rs = B_(nc.sbuf_tensor("rs", [128, HT * CP], F32))
            hq = [B_(nc.sbuf_tensor("hq%d" % i, [128, HT * CP], BF16))
                  for i in range(2)]
            bstT = B_(nc.sbuf_tensor("bstT", [128, 12], F32))
            mvT = B_(nc.sbuf_tensor("mvT", [128, 2], F32))
            abtT = B_(nc.sbuf_tensor("abtT", [128, 8], F32))
            abrowT = B_(nc.sbuf_tensor("abrowT", [1, 256], F32))
            bcT = B_(nc.sbuf_tensor("bcT", [128, 1024], F32))
            bcTQ = [B_(nc.sbuf_tensor("bcTQ%d" % i, [128, 1024], F32))
                    for i in range(2)]
            abtQ = B_(nc.sbuf_tensor("abtQ", [128, 8], F32))
            abrowQ = B_(nc.sbuf_tensor("abrowQ", [1, 256], F32))

            sb.dma("sync", w1q_s[:].rearrange("p (t c) -> p t c", c=DH),
                   dw1q.rearrange("(t p) c -> p t c", p=128), reads=(dw1q,),
                   writes=(w1q_s,))
            sb.dma("sync", w2q_s[:].rearrange("p (t c) -> p t c", c=D),
                   dw2q.rearrange("(t p) c -> p t c", p=128), reads=(dw2q,),
                   writes=(w2q_s,))
            for t3 in range(3):
                sb.dma("sync", cw_s[t3][:], dcw[t3], writes=(cw_s[t3],))
            sb.dma("sync", cb_s[:], dcb, writes=(cb_s,))

            def ln_chunk(srct, pbA, pbB, bst, mv, abt, abrow_t, bc_t):
                for k in range(KT):
                    bank = pbA if k < 4 else pbB
                    sb.op("tensor", lambda t, k=k, bank=bank: t.transpose(
                        bank[:, 128 * (k % 4):128 * (k % 4 + 1)],
                        srct[:, k * CP:(k + 1) * CP], ident[:]),
                        reads=(srct, ident), writes=(bank,), inc=(k % 4 == 3))
                yield
                sb.op("vector", lambda v: v.bn_stats(
                    out=bst[:, 0:6], in_=pbA[:, 0:512]), reads=(pbA,), writes=(bst,))
                sb.op("vector", lambda v: v.bn_stats(
                    out=bst[:, 6:12], in_=pbB[:, 0:512]), reads=(pbB,), writes=(bst,))
                sb.op("vector", lambda v: v.bn_aggr(
                    out=mv[:, 0:2], in_=bst[:, 0:12]), reads=(bst,), writes=(mv,))
                sb.op("vector", lambda v: v.tensor_scalar(
                    out=abt[:, 2:3], in0=mv[:, 1:2], scalar1=1e-5, scalar2=None,
                    op0=ALU.add), reads=(mv,), writes=(abt,))
                sb.op("scalar", lambda s: s.activation(
                    out=abt[:, 3:4], in_=abt[:, 2:3], func=AF.Sqrt),
                    reads=(abt,), writes=(abt,))
                sb.op("vector", lambda v: v.reciprocal(
                    out=abt[:, 0:1], in_=abt[:, 3:4]), reads=(abt,), writes=(abt,))
                sb.op("vector", lambda v: v.tensor_scalar(
                    out=abt[:, 4:5], in0=mv[:, 0:1], scalar1=-1.0, scalar2=None,
                    op0=ALU.mult), reads=(mv,), writes=(abt,))
                sb.op("vector", lambda v: v.tensor_tensor(
                    out=abt[:, 1:2], in0=abt[:, 4:5], in1=abt[:, 0:1],
                    op=ALU.mult), reads=(abt,), writes=(abt,))
                yield
                bc2_pair(abt, 0, 1, abrow_t, bc_t, pbA)

            def bc2_pair(abt, c0, c1, abrow_t, bc_t, pbX):
                for i, cc in enumerate((c0, c1)):
                    sb.op("tensor", lambda t, i=i, cc=cc, abt=abt: t.transpose(
                        pbX[0:1, 16 + 128 * i:144 + 128 * i], abt[:, cc:cc + 1],
                        ident[:]), reads=(abt, ident), writes=(pbX,), inc=(i == 1))
                sb.op("vector", lambda v: v.tensor_copy(
                    out=abrow_t[0:1, 0:256], in_=pbX[0:1, 16:272]),
                    reads=(pbX,), writes=(abrow_t,))
                for i in range(4):
                    sb.op("tensor", lambda t, i=i: t.matmul(
                        pbX[:, 128 * i:128 * (i + 1)], ones[0:1, 0:128],
                        abrow_t[0:1, 128 * (i // 2):128 * (i // 2 + 1)],
                        start=True, stop=True),
                        reads=(ones, abrow_t), writes=(pbX,), inc=(i == 3))
                for j, dst in ((0, 0), (1, 512)):
                    sb.op("vector", lambda v, j=j, dst=dst: v.tensor_copy(
                        out=bc_t[:, dst:dst + 256], in_=pbX[:, 256 * j:256 * (j + 1)]),
                        reads=(pbX,), writes=(bc_t,))
                    sb.op("vector", lambda v, dst=dst: v.tensor_copy(
                        out=bc_t[:, dst + 256:dst + 512], in_=bc_t[:, dst:dst + 256]),
                        reads=(bc_t,), writes=(bc_t,))

            def qscale(maxcol_ap, maxcol_t, pcol, pbX, abt, abrow_t, bc_t):
                sb.op("tensor", lambda t: t.transpose(
                    pbX[:, 384:512], maxcol_ap, ident[:]),
                    reads=(maxcol_t, ident), writes=(pbX,))
                yield
                sb.op("vector", lambda v, abt=abt: v.tensor_reduce(
                    out=abt[:, 5:6], in_=pbX[:, 384:512], axis=AX.X, op=ALU.max),
                    reads=(pbX,), writes=(abt,))
                sb.op("vector", lambda v, abt=abt: v.tensor_scalar(
                    out=abt[:, 5:6], in0=abt[:, 5:6], scalar1=1e-5, scalar2=None,
                    op0=ALU.max), reads=(abt,), writes=(abt,))
                sb.op("vector", lambda v, abt=abt: v.reciprocal(
                    out=abt[:, 6:7], in_=abt[:, 5:6]), reads=(abt,), writes=(abt,))
                sb.op("vector", lambda v, abt=abt: v.tensor_scalar(
                    out=abt[:, 7:8], in0=abt[:, 6:7], scalar1=127.0, scalar2=None,
                    op0=ALU.mult), reads=(abt,), writes=(abt,))
                sb.op("vector", lambda v, abt=abt, pcol=pcol: v.tensor_scalar(
                    out=abt[:, 6:7], in0=abt[:, 5:6], scalar1=pers[:, pcol:pcol + 1],
                    scalar2=None, op0=ALU.mult), reads=(abt, pers), writes=(abt,))
                yield
                bc2_pair(abt, 7, 6, abrow_t, bc_t, pbX)

            hb3 = [hbuf[i][:].rearrange("p (t c) -> p t c", c=CP + 2) for i in range(3)]
            hs3 = hs[:].rearrange("p (t c) -> p t c", c=CP)

            def load_bhead(ci):
                sb.dma("sync", xnqc[ci % 2][:].rearrange("p (t n) -> p t n", n=CP),
                       dxnq.rearrange("(t p) n -> p t n", p=128)[:, :, ci * CP:(ci + 1) * CP],
                       reads=(dxnq,), writes=(xnqc[ci % 2],))
                sb.dma("sync", invc4[ci % 2][:, 0:128], dinv[:, ci * CP:(ci + 1) * CP],
                       reads=(dinv,), writes=(invc4[ci % 2],))

            if 'B' in phases:
                sb.dma("sync", x2b[0][:].rearrange("p (t n) -> p t n", n=CP),
                       dx2.rearrange("(t p) n -> p t n", p=128)[:, :, 0:CP],
                       reads=(dx2,), writes=(x2b[0],))
                load_bhead(0)
            def head_stages(c):
                if True:
                    sb.op("vector", lambda v, c=c: v.tensor_copy(
                        out=invc4[c % 2][:, 128:256], in_=invc4[c % 2][:, 0:128]),
                        reads=(invc4[c % 2],), writes=(invc4[c % 2],))
                    sb.op("vector", lambda v, c=c: v.tensor_copy(
                        out=invc4[c % 2][:, 256:512], in_=invc4[c % 2][:, 0:256]),
                        reads=(invc4[c % 2],), writes=(invc4[c % 2],))
                    yield
                    hb = hbuf[c % 3]
                    for grp in range(8):
                        bank = pb[grp % 2]
                        for sl in range(4):
                            ot = 4 * grp + sl
                            for k in range(KT):
                                sb.op("tensor",
                                      lambda t, ot=ot, sl=sl, k=k, bank=bank, c=c: t.matmul(
                                          bank[:, 128 * sl:128 * sl + CP],
                                          w1q_s[:, DH * k + 128 * ot:DH * k + 128 * (ot + 1)],
                                          xnqc[c % 2][:, k * CP:(k + 1) * CP],
                                          start=(k == 0), stop=(k == KT - 1)),
                                      reads=(w1q_s, xnqc[c % 2]), writes=(bank,),
                                      inc=(sl == 3 and k == KT - 1))
                        sb.op("vector", lambda v, grp=grp, bank=bank, hbi=c % 3, c=c: v.tensor_tensor(
                            out=hb3[hbi][:, 4 * grp:4 * grp + 4, 1:1 + CP],
                            in0=bank[:, 0:512].rearrange("p (s c) -> p s c", c=128),
                            in1=invc4[c % 2][:, 0:512].rearrange("p (s c) -> p s c", c=128),
                            op=ALU.mult), reads=(bank, invc4[c % 2]), writes=(hb,))
                        if grp == 3:
                            yield
                    yield
                    if c == 0:
                        sb.op("vector", lambda v: v.memset(
                            hb3[0][:, :, 0:1], 0.0), writes=(hb,))
                    else:
                        sb.op("vector", lambda v, ci=c: v.tensor_copy(
                            out=hb3[ci % 3][:, :, 0:1],
                            in_=hb3[(ci - 1) % 3][:, :, CP:CP + 1]),
                            reads=(hbuf[(c - 1) % 3],), writes=(hb,))
            def tail_pre_stages(cm):
                hb = hbuf[cm % 3]
                hbv = hb3[cm % 3]
                rs3 = rs[:].rearrange("p (t c) -> p t c", c=CP)
                bcq = bcTQ[cm % 2]
                hqc = hq[cm % 2]
                for t32 in range(HT):
                    sb.op("scalar", lambda s, t=t32: s.activation(
                        out=hs3[:, t:t + 1, :], in_=hbv[:, t:t + 1, 0:CP],
                        func=AF.Identity, scale=cw_s[0][:, t:t + 1],
                        bias=cb_s[:, t:t + 1]),
                        reads=(hb, cw_s[0], cb_s), writes=(hs,))
                    sb.op("vector", lambda v, t=t32: v.scalar_tensor_tensor(
                        out=hs3[:, t:t + 1, :], in0=hbv[:, t:t + 1, 1:1 + CP],
                        scalar=cw_s[1][:, t:t + 1], in1=hs3[:, t:t + 1, :],
                        op0=ALU.mult, op1=ALU.add),
                        reads=(hb, cw_s[1], hs), writes=(hs,))
                    if t32 == 15:
                        yield
                yield
                if cm + 1 < NCH:
                    sb.op("vector", lambda v, cm=cm: v.tensor_copy(
                        out=hb3[cm % 3][:, :, CP + 1:CP + 2],
                        in_=hb3[(cm + 1) % 3][:, :, 1:2]),
                        reads=(hbuf[(cm + 1) % 3],), writes=(hb,))
                else:
                    sb.op("vector", lambda v, cm=cm: v.memset(
                        hb3[cm % 3][:, :, CP + 1:CP + 2], 0.0), writes=(hb,))
                for t32 in range(HT):
                    sb.op("scalar", lambda s, t=t32: s.activation(
                        out=rs3[:, t:t + 1, :], in_=hbv[:, t:t + 1, 2:2 + CP],
                        func=AF.Identity, scale=cw_s[2][:, t:t + 1]),
                        reads=(hb, cw_s[2]), writes=(rs,))
                    if t32 == 15:
                        yield
                yield
                for g in range(4):
                    sb.op("vector", lambda v, g=g: v.tensor_tensor(
                        out=hs[:, 1024 * g:1024 * (g + 1)],
                        in0=hs[:, 1024 * g:1024 * (g + 1)],
                        in1=rs[:, 1024 * g:1024 * (g + 1)], op=ALU.add),
                        reads=(hs, rs), writes=(hs,))
                yield
                for g8 in range(4):
                    sb.op("scalar", lambda s, g8=g8: s.activation(
                        out=hs[:, 1024 * g8:1024 * (g8 + 1)],
                        in_=hs[:, 1024 * g8:1024 * (g8 + 1)], func=AF.Silu),
                        reads=(hs,), writes=(hs,))
                sb.op("scalar", lambda s: s.activation(
                    out=rs[:], in_=hs[:], func=AF.Abs), reads=(hs,), writes=(rs,))
                yield
                for span in (2048, 1024, 512, 256, 128):
                    sb.op("vector", lambda v, span=span: v.tensor_tensor(
                        out=rs[:, 0:span], in0=rs[:, 0:span],
                        in1=rs[:, span:2 * span], op=ALU.max),
                        reads=(rs,), writes=(rs,))
                yield
                yield from qscale(rs[:, 0:128], rs, 3, pb[4], abtQ, abrowQ, bcq)
                yield
                for g in range(8):
                    sb.op("vector", lambda v, g=g, bcq=bcq: v.tensor_tensor(
                        out=hs[:, 512 * g:512 * (g + 1)],
                        in0=hs[:, 512 * g:512 * (g + 1)],
                        in1=bcq[:, 0:512], op=ALU.mult),
                        reads=(hs, bcq), writes=(hs,))
                sb.op("vector", lambda v: v.tensor_scalar(
                    out=rs[:], in0=hs[:], scalar1=C32, scalar2=C32,
                    op0=ALU.add, op1=ALU.subtract), reads=(hs,), writes=(rs,))
                sb.op("vector", lambda v: v.tensor_scalar(
                    out=hs[:], in0=hs[:], scalar1=C52, scalar2=C52,
                    op0=ALU.add, op1=ALU.subtract), reads=(hs,), writes=(hs,))
                sb.op("vector", lambda v, hqc=hqc: v.scalar_tensor_tensor(
                    out=hqc[:], in0=rs[:], scalar=flagg[:, 1:2], in1=hs[:],
                    op0=ALU.mult, op1=ALU.add),
                    reads=(rs, hs, flagg), writes=(hqc,))

            def tail_post_stages(cm):
                hb = hbuf[cm % 3]
                bcq = bcTQ[cm % 2]
                hqc = hq[cm % 2]
                for ot in range(8):
                    bank = pb[2 + ot % 2]
                    for k in range(HT):
                        sb.op("tensor", lambda t, ot=ot, k=k, bank=bank, hqc=hqc: t.matmul(
                            bank[:, 0:CP],
                            w2q_s[:, D * k + 128 * ot:D * k + 128 * (ot + 1)],
                            hqc[:, k * CP:(k + 1) * CP],
                            start=(k == 0), stop=(k == HT - 1)),
                            reads=(w2q_s, hqc), writes=(bank,), inc=(k == HT - 1))
                    sb.op("vector", lambda v, ot=ot, bank=bank, bcq=bcq: v.tensor_tensor(
                        out=fv[:, 128 * ot:128 * (ot + 1)], in0=bank[:, 0:CP],
                        in1=bcq[:, 512:640], op=ALU.mult),
                        reads=(bank, bcq), writes=(fv,))
                    if ot == 3:
                        yield
                yield
                yield from ln_chunk(fv, pb[6], pb[7], bstT, mvT, abtT, abrowT, bcT)
                yield
                x2old = x2b[cm % 4]
                for g in range(2):
                    sb.op("vector", lambda v, g=g: v.tensor_tensor(
                        out=fv[:, 512 * g:512 * (g + 1)],
                        in0=fv[:, 512 * g:512 * (g + 1)],
                        in1=bcT[:, 0:512], op=ALU.mult),
                        reads=(fv, bcT), writes=(fv,))
                for g in range(2):
                    sb.op("vector", lambda v, g=g: v.tensor_tensor(
                        out=fv[:, 512 * g:512 * (g + 1)],
                        in0=fv[:, 512 * g:512 * (g + 1)],
                        in1=bcT[:, 512:1024], op=ALU.add),
                        reads=(fv, bcT), writes=(fv,))
                for g in range(2):
                    sb.op("vector", lambda v, g=g, x2old=x2old: v.tensor_tensor(
                        out=fv[:, 512 * g:512 * (g + 1)],
                        in0=fv[:, 512 * g:512 * (g + 1)],
                        in1=x2old[:, 512 * g:512 * (g + 1)], op=ALU.add),
                        reads=(fv, x2old), writes=(fv,))
                sb.dma("gpsimd",
                       dout.rearrange("(t p) n -> p t n", p=128)[:, :, cm * CP:(cm + 1) * CP],
                       fv[:].rearrange("p (t n) -> p t n", n=CP),
                       reads=(fv,), writes=(dout,))
                if dbg and cm == NCH - 1:
                    sb.dma("gpsimd", ddbg["hbuf"], hb[:], reads=(hb,))
                    sb.dma("gpsimd", ddbg["hs"], hs[:], reads=(hs,))
                    sb.dma("gpsimd", ddbg["gam"], pers[:], reads=(pers,))

            for c in range((NCH + 2) if 'B' in phases else 0):
                if c + 1 < NCH:
                    sb.dma("sync", x2b[(c + 1) % 4][:].rearrange("p (t n) -> p t n", n=CP),
                           dx2.rearrange("(t p) n -> p t n", p=128)[
                               :, :, (c + 1) * CP:(c + 2) * CP],
                           reads=(dx2,), writes=(x2b[(c + 1) % 4],))
                    load_bhead(c + 1)
                import os as _os
                post = tail_post_stages(c - 2) if 0 <= c - 2 else None
                pre = tail_pre_stages(c - 1) if 0 <= c - 1 < NCH else None
                if _os.environ.get("NOINTER") == "1":
                    if post is not None:
                        for _ in post:
                            pass
                    if c < NCH:
                        for _ in head_stages(c):
                            pass
                    if pre is not None:
                        for _ in pre:
                            pass
                else:
                    if post is not None:
                        for _ in range(2):
                            next(post, None)
                    if pre is not None:
                        next(pre, None)
                    if c < NCH:
                        for _ in head_stages(c):
                            pass
                    gens = [g for g in (pre, post) if g is not None]
                    while gens:
                        for ggen in list(gens):
                            try:
                                next(ggen)
                            except StopIteration:
                                gens.remove(ggen)
            sb.barrier()
    return nc


def host_inputs(qkv_w, qkv_b, out_w, out_b, w1, conv_w, conv_b, w2):
    qkv_w = np.asarray(qkv_w, dtype=np.float32)
    qkv_b = np.asarray(qkv_b, dtype=np.float32)
    out_w = np.asarray(out_w, dtype=np.float32)
    out_b = np.asarray(out_b, dtype=np.float32)
    w1 = np.asarray(w1, dtype=np.float32)
    conv_w = np.asarray(conv_w, dtype=np.float32)
    conv_b = np.asarray(conv_b, dtype=np.float32)
    w2 = np.asarray(w2, dtype=np.float32)
    ins = {}
    ins["qkvT"] = np.ascontiguousarray(qkv_w.T)
    ins["qkvb"] = np.ascontiguousarray(qkv_b[:2048].reshape(16, 128).T)
    ins["vbias_bc"] = np.ascontiguousarray(np.broadcast_to(qkv_b[2048:], (128, 1024)))
    ins["outT"] = np.ascontiguousarray(out_w.T)
    ins["outb"] = np.ascontiguousarray(out_b.reshape(8, 128).T)
    ins["w1T"] = np.ascontiguousarray(w1.T)
    ins["w2T"] = np.ascontiguousarray(w2.T)
    cw = conv_w.reshape(DH, 3)
    for t in range(3):
        ins["convw%d" % t] = np.ascontiguousarray(cw[:, t].reshape(HT, 128).T)
    ins["convb"] = np.ascontiguousarray(conv_b.reshape(HT, 128).T)
    mask1 = np.triu(np.ones((128, 128), dtype=np.float32))
    ins["mask4d"] = np.ascontiguousarray(np.tile(mask1, (1, 4)))
    ins["identd"] = np.eye(128, dtype=np.float32)
    ins["onesd"] = np.ones((128, 128), dtype=np.float32)
    return ins


def kernel(L0_attn_x, L0_attn_qkv_w, L0_attn_qkv_b, L0_attn_out_w, L0_attn_out_b,
           L0_ffn_w1, L0_ffn_conv_w, L0_ffn_conv_b, L0_ffn_w2):
    x = np.asarray(L0_attn_x, dtype=np.float32)
    B = x.shape[0]
    ins = host_inputs(L0_attn_qkv_w, L0_attn_qkv_b, L0_attn_out_w, L0_attn_out_b,
                      L0_ffn_w1, L0_ffn_conv_w, L0_ffn_conv_b, L0_ffn_w2)
    nc = build_program(T=x.shape[1])
    in_maps = []
    for b in range(B):
        m = dict(ins)
        m["x"] = np.ascontiguousarray(x[b].T)
        in_maps.append(m)
    res = run_bass_kernel_spmd(nc, in_maps, core_ids=list(range(B)))
    out = np.stack([np.asarray(r["out"]).T for r in res.results], axis=0)
    return out


# version 19
# speedup vs baseline: 1.0540x; 1.0540x over previous
import contextlib
import os
import numpy as np
import concourse.bass as bass
import concourse.mybir as mybir
from concourse.bass_utils import run_bass_kernel_spmd

F32 = mybir.dt.float32
BF16 = mybir.dt.bfloat16
FP8 = mybir.dt.float8e4
AF = mybir.ActivationFunctionType
ALU = mybir.AluOpType
AX = mybir.AxisListType

C32 = float(np.float32(1.5 * 2 ** 23))
C52 = float(np.float32(1.5 * 2 ** 52))

D = 1024
H = 16
DH = 4096
KT = 8
HT = 32
CP = 128

PENDING = -1


class Sb:

    ENGS = ("tensor", "vector", "scalar", "gpsimd", "sync")
    NDSEM = 16

    def __init__(self, nc, ctx):
        self.nc = nc
        self.eng = {"tensor": nc.tensor, "vector": nc.vector, "scalar": nc.scalar,
                    "gpsimd": nc.gpsimd, "sync": nc.sync}
        self.sem = {}
        for name in self.ENGS:
            self.sem[name] = ctx.enter_context(nc.semaphore("sb_" + name))
        for q in ("dsync", "dgp"):
            for i in range(self.NDSEM):
                nm = "%s%d" % (q, i)
                self.sem[nm] = ctx.enter_context(nc.semaphore("sb_" + nm))
        self.cnt = {name: 0 for name in self.sem}
        self.step = {name: (16 if name.startswith("d") else 1) for name in self.sem}
        self.qn = {"dsync": 0, "dgp": 0}
        self.lastw = {}
        self.readers = {}
        self.water = {}
        self.pending = {e: [] for e in self.ENGS}

    def _flush(self, src):
        self.cnt[src] += 1
        self.eng[src].nop().then_inc(self.sem[src], 1)
        self._resolve(src)

    def _resolve(self, src):
        c = self.cnt[src]
        for key, kind in self.pending[src]:
            if kind == "w":
                s, v = self.lastw.get(key, (None, None))
                if s == src and v == PENDING:
                    self.lastw[key] = (src, c)
            else:
                d = self.readers.get(key)
                if d is not None and d.get(src) == PENDING:
                    d[src] = c
        self.pending[src] = []

    def _wait(self, eng, src, cnt):
        if cnt == PENDING:
            if src == eng:
                return
            self._flush(src)
            cnt = self.cnt[src]
        if cnt <= self.water.get((eng, src), 0):
            return
        self.eng[eng].wait_ge(self.sem[src], cnt * self.step[src])
        self.water[(eng, src)] = cnt

    def _pre(self, eng, reads, writes):
        for r in reads:
            lw = self.lastw.get(id(r))
            if lw is not None:
                self._wait(eng, lw[0], lw[1])
        for w in writes:
            lw = self.lastw.get(id(w))
            if lw is not None:
                self._wait(eng, lw[0], lw[1])
            for src, c in list(self.readers.get(id(w), {}).items()):
                self._wait(eng, src, c)

    def _post(self, src, reads, writes, cnt):
        for w in writes:
            self.lastw[id(w)] = (src, cnt)
            self.readers[id(w)] = {}
            if cnt == PENDING:
                self.pending[src].append((id(w), "w"))
        for r in reads:
            self.readers.setdefault(id(r), {})[src] = cnt
            if cnt == PENDING:
                self.pending[src].append((id(r), "r"))

    budget = [10**9]

    def op(self, eng, emit, reads=(), writes=(), inc=True):
        if Sb.budget[0] <= 0:
            return None
        Sb.budget[0] -= 1
        if Sb.budget[0] == 0:
            inc = True
        self._pre(eng, reads, writes)
        inst = emit(self.eng[eng])
        if inc:
            self.cnt[eng] += 1
            inst.then_inc(self.sem[eng], 1)
            self._resolve(eng)
            c = self.cnt[eng]
        else:
            c = PENDING
        self._post(eng, reads, writes, c)
        return inst

    def dma(self, queue, out, in_, reads=(), writes=()):
        if Sb.budget[0] <= 0:
            return None
        Sb.budget[0] -= 1
        q = {"sync": "dsync", "gpsimd": "dgp"}[queue]
        n = self.qn[q]
        self.qn[q] = n + 1
        src = "%s%d" % (q, n % self.NDSEM)
        self._pre(queue, reads, writes)
        inst = self.eng[queue].dma_start(out=out, in_=in_)
        self.cnt[src] += 1
        inst.then_inc(self.sem[src], 16)
        self._post(src, reads, writes, self.cnt[src])

    def barrier(self):
        for e in self.ENGS:
            if self.pending[e]:
                self._flush(e)
        for e in self.ENGS:
            for s in self.sem:
                if s != e:
                    self._wait(e, s, self.cnt[s])


def build_program(T=2048, dbg=False, phases="WAB", a_stages=99, b_stages=99):
    nc = bass.Bass("TRN2", target_bir_lowering=False, detect_race_conditions=False)
    NCH = T // CP
    GRP = min(512, T)
    SUB = GRP // CP
    NG = T // GRP

    dx = nc.dram_tensor("x", [D, T], F32, kind="ExternalInput").ap()
    dqkvT = nc.dram_tensor("qkvT", [D, 3 * D], F32, kind="ExternalInput").ap()
    dqkvb = nc.dram_tensor("qkvb", [128, 16], F32, kind="ExternalInput").ap()
    dvbias = nc.dram_tensor("vbias_bc", [128, D], F32, kind="ExternalInput").ap()
    doutT = nc.dram_tensor("outT", [D, D], F32, kind="ExternalInput").ap()
    doutb = nc.dram_tensor("outb", [128, 8], F32, kind="ExternalInput").ap()
    dw1T = nc.dram_tensor("w1T", [D, DH], F32, kind="ExternalInput").ap()
    dw2T = nc.dram_tensor("w2T", [DH, D], F32, kind="ExternalInput").ap()
    dcw = [nc.dram_tensor("convw%d" % t, [128, HT], F32, kind="ExternalInput").ap()
           for t in range(3)]
    dcb = nc.dram_tensor("convb", [128, HT], F32, kind="ExternalInput").ap()
    dmask = nc.dram_tensor("mask4d", [128, 512], F32, kind="ExternalInput").ap()
    dident = nc.dram_tensor("identd", [128, 128], F32, kind="ExternalInput").ap()
    dones = nc.dram_tensor("onesd", [128, 128], F32, kind="ExternalInput").ap()

    dx2 = nc.dram_tensor("x2d", [D, T], F32).ap()
    dqk = nc.dram_tensor("qkd", [2 * D, T], F32).ap()
    dxnq = nc.dram_tensor("xnqd", [D, T], BF16).ap()
    dinv = nc.dram_tensor("invd", [128, T], F32).ap()
    dvt = nc.dram_tensor("vtd", [T, 1040], F32).ap()
    dw1q = nc.dram_tensor("w1qd", [D, DH], FP8).ap()
    dw2q = nc.dram_tensor("w2qd", [DH, D], FP8).ap()
    dout = nc.dram_tensor("out", [D, T], F32, kind="ExternalOutput").ap()

    ddbg = {}
    if dbg:
        for nm, shp in (("xn", [128, KT * CP]), ("qkfm", [128, 16 * CP]),
                        ("vtm", [128, H * 65]), ("kvext", [128, 520]),
                        ("aofm", [128, KT * CP]), ("po", [128, KT * CP]),
                        ("hbuf", [128, HT * (CP + 2)]), ("hs", [128, HT * CP]),
                        ("gam", [128, 4]), ("xnq", [128, KT * CP]),
                        ("x2c", [128, KT * GRP])):
            ddbg[nm] = nc.dram_tensor("dbg_" + nm, shp, F32, kind="ExternalOutput").ap()

    with contextlib.ExitStack() as ctx:
        sb = Sb(nc, ctx)

        ident = ctx.enter_context(nc.sbuf_tensor("ident", [128, 128], F32))
        ones = ctx.enter_context(nc.sbuf_tensor("ones", [128, 128], F32))
        flagg = ctx.enter_context(nc.sbuf_tensor("flagg", [128, 2], F32))
        pers = ctx.enter_context(nc.sbuf_tensor("pers", [128, 4], F32))
        pb = [ctx.enter_context(nc.psum_tensor("pb%d" % i, [128, 512], F32))
              for i in range(8)]

        ctx.enter_context(nc.Block())

        sb.dma("sync", ident[:], dident, writes=(ident,))
        sb.dma("sync", ones[:], dones, writes=(ones,))

        sb.op("vector", lambda v: v.tensor_scalar(
            out=flagg[:, 0:1], in0=ones[:, 0:1], scalar1=float(2.0 ** -28),
            scalar2=C32, op0=ALU.mult, op1=ALU.add), reads=(ones,), writes=(flagg,))
        sb.op("vector", lambda v: v.tensor_scalar(
            out=flagg[:, 0:1], in0=flagg[:, 0:1], scalar1=C32,
            scalar2=float(2.0 ** 28), op0=ALU.subtract, op1=ALU.mult),
            reads=(flagg,), writes=(flagg,))
        sb.op("vector", lambda v: v.tensor_scalar(
            out=flagg[:, 1:2], in0=flagg[:, 0:1], scalar1=-1.0, scalar2=1.0,
            op0=ALU.mult, op1=ALU.add), reads=(flagg,), writes=(flagg,))

        WCH = 2048

        def make_w_stream(A):
            wch = [A(nc.sbuf_tensor("wch%d" % i, [128, WCH], F32)) for i in range(2)]
            sAc = A(nc.sbuf_tensor("sAc", [128, WCH], F32))
            stgc = [A(nc.sbuf_tensor("stgc%d" % i, [128, WCH], FP8)) for i in range(2)]
            parts = A(nc.sbuf_tensor("wparts", [128, 36], F32))
            gscw = A(nc.sbuf_tensor("gscw", [1, 4], F32))

            def chunk_aps(dfull, n_cols, ci, sbt):
                if n_cols >= WCH:
                    spt = n_cols // WCH
                    return (dfull[:, ci // spt, WCH * (ci % spt):WCH * (ci % spt) + WCH],
                            sbt[:])
                tpc = WCH // n_cols
                return (dfull[:, tpc * ci:tpc * ci + tpc, :],
                        sbt[:].rearrange("p (t c) -> p t c", c=n_cols))

            def gen():
                for wi, (dwf, dwq, n_tiles, n_cols, pcol) in enumerate(
                        ((dw1T, dw1q, KT, DH, 0), (dw2T, dw2q, HT, D, 2))):
                    dfull = dwf.rearrange("(t p) c -> p t c", p=128)
                    dqfull = dwq.rearrange("(t p) c -> p t c", p=128)
                    nch = (n_tiles * n_cols) // WCH
                    base = 16 * wi
                    for ci in range(nch):
                        dap, sap = chunk_aps(dfull, n_cols, ci, wch[ci % 2])
                        sb.dma("sync", sap, dap, reads=(dwf,), writes=(wch[ci % 2],))
                        yield "d"
                        sb.op("vector", lambda v, ci=ci: v.reduce_sum(
                            out=parts[:, base + ci:base + ci + 1], in_=wch[ci % 2][:],
                            axis=AX.X, apply_absolute_value=True),
                            reads=(wch[ci % 2],), writes=(parts,))
                        yield "v"
                    sb.op("vector", lambda v, nch=nch, base=base: v.reduce_sum(
                        out=parts[:, 32 + wi:33 + wi], in_=parts[:, base:base + nch],
                        axis=AX.X), reads=(parts,), writes=(parts,))
                    sb.op("tensor", lambda t, wi=wi: t.matmul(
                        pb[0][0:1, 0:1], parts[:, 32 + wi:33 + wi], ones[:, 0:1],
                        start=True, stop=True), reads=(parts, ones), writes=(pb[0],))
                    sb.op("vector", lambda v, n_tiles=n_tiles, n_cols=n_cols: v.tensor_scalar(
                        out=gscw[0:1, 0:1], in0=pb[0][0:1, 0:1],
                        scalar1=1.0 / (n_tiles * 128 * n_cols), scalar2=1e-5,
                        op0=ALU.mult, op1=ALU.max), reads=(pb[0],), writes=(gscw,))
                    sb.op("vector", lambda v: v.reciprocal(
                        out=gscw[0:1, 1:2], in_=gscw[0:1, 0:1]),
                        reads=(gscw,), writes=(gscw,))
                    sb.op("vector", lambda v: v.tensor_scalar(
                        out=gscw[0:1, 2:3], in0=gscw[0:1, 0:1], scalar1=1.0 / 127.0,
                        scalar2=None, op0=ALU.mult), reads=(gscw,), writes=(gscw,))
                    for src_col, dst_col in ((1, pcol), (2, pcol + 1)):
                        sb.op("tensor", lambda t, sc=src_col: t.matmul(
                            pb[0][:, 0:1], ones[0:1, 0:128], gscw[0:1, sc:sc + 1],
                            start=True, stop=True), reads=(ones, gscw), writes=(pb[0],))
                        sb.op("vector", lambda v, dc=dst_col: v.tensor_copy(
                            out=pers[:, dc:dc + 1], in_=pb[0][:, 0:1]),
                            reads=(pb[0],), writes=(pers,))
                    yield "v"
                    for ci in range(nch):
                        dap, sap = chunk_aps(dfull, n_cols, ci, wch[ci % 2])
                        dqap, sqap = chunk_aps(dqfull, n_cols, ci, stgc[ci % 2])
                        sb.dma("sync", sap, dap, reads=(dwf,), writes=(wch[ci % 2],))
                        yield "d"
                        sb.op("vector", lambda v, ci=ci, pcol=pcol: v.tensor_scalar(
                            out=sAc[:], in0=wch[ci % 2][:], scalar1=pers[:, pcol:pcol + 1],
                            scalar2=0.5, op0=ALU.mult, op1=ALU.is_gt),
                            reads=(wch[ci % 2], pers), writes=(sAc,))
                        yield "v"
                        sb.op("vector", lambda v, ci=ci, pcol=pcol: v.tensor_scalar(
                            out=wch[ci % 2][:], in0=wch[ci % 2][:],
                            scalar1=pers[:, pcol:pcol + 1],
                            scalar2=-0.5, op0=ALU.mult, op1=ALU.is_lt),
                            reads=(wch[ci % 2], pers), writes=(wch[ci % 2],))
                        yield "v"
                        sb.op("vector", lambda v, ci=ci: v.tensor_tensor(
                            out=stgc[ci % 2][:], in0=sAc[:], in1=wch[ci % 2][:],
                            op=ALU.subtract),
                            reads=(sAc, wch[ci % 2]), writes=(stgc[ci % 2],))
                        yield "v"
                        sb.dma("gpsimd", dqap, sqap,
                               reads=(stgc[ci % 2],), writes=(dwq,))
                        yield "d"
            return gen()

        def make_ln(xsq_t, stt_t, abtm_t, abrow_t, bc_t):
            def emit_ln(src):
                sb.op("scalar", lambda s: s.activation(
                    out=xsq_t[:], in_=src[:], func=AF.Square),
                    reads=(src,), writes=(xsq_t,))
                for col, ten in ((0, src), (1, xsq_t)):
                    for k in range(KT):
                        sb.op("tensor", lambda t, k=k, col=col, ten=ten: t.matmul(
                            pb[4][:, col:col + 1], ten[:, k * CP:(k + 1) * CP],
                            ones[:, 0:1], start=(k == 0), stop=(k == KT - 1)),
                            reads=(ten, ones), writes=(pb[4],), inc=(k == KT - 1))
                sb.op("vector", lambda v: v.tensor_scalar(
                    out=stt_t[:, 0:2], in0=pb[4][:, 0:2], scalar1=1.0 / D,
                    scalar2=None, op0=ALU.mult), reads=(pb[4],), writes=(stt_t,))
                sb.op("vector", lambda v: v.tensor_tensor(
                    out=stt_t[:, 2:3], in0=stt_t[:, 0:1], in1=stt_t[:, 0:1],
                    op=ALU.mult), reads=(stt_t,), writes=(stt_t,))
                sb.op("vector", lambda v: v.tensor_scalar(
                    out=stt_t[:, 2:3], in0=stt_t[:, 2:3], scalar1=-1.0, scalar2=1e-5,
                    op0=ALU.mult, op1=ALU.add), reads=(stt_t,), writes=(stt_t,))
                sb.op("vector", lambda v: v.tensor_tensor(
                    out=stt_t[:, 3:4], in0=stt_t[:, 1:2], in1=stt_t[:, 2:3],
                    op=ALU.add), reads=(stt_t,), writes=(stt_t,))
                sb.op("scalar", lambda s: s.activation(
                    out=stt_t[:, 4:5], in_=stt_t[:, 3:4], func=AF.Sqrt),
                    reads=(stt_t,), writes=(stt_t,))
                sb.op("vector", lambda v: v.reciprocal(
                    out=abtm_t[:, 0:1], in_=stt_t[:, 4:5]),
                    reads=(stt_t,), writes=(abtm_t,))
                sb.op("vector", lambda v: v.tensor_scalar(
                    out=stt_t[:, 5:6], in0=stt_t[:, 0:1], scalar1=-1.0, scalar2=None,
                    op0=ALU.mult), reads=(stt_t,), writes=(stt_t,))
                sb.op("vector", lambda v: v.tensor_tensor(
                    out=abtm_t[:, 1:2], in0=stt_t[:, 5:6], in1=abtm_t[:, 0:1],
                    op=ALU.mult), reads=(stt_t, abtm_t), writes=(abtm_t,))
                for i in range(2):
                    sb.op("tensor", lambda t, i=i: t.transpose(
                        pb[4][0:1, 16 + 128 * i:144 + 128 * i], abtm_t[:, i:i + 1],
                        ident[:]), reads=(abtm_t, ident), writes=(pb[4],),
                        inc=(i == 1))
                sb.op("vector", lambda v: v.tensor_copy(
                    out=abrow_t[0:1, 0:256], in_=pb[4][0:1, 16:272]),
                    reads=(pb[4],), writes=(abrow_t,))
                for i in range(2):
                    sb.op("tensor", lambda t, i=i: t.matmul(
                        pb[4][:, 256 + 128 * i:384 + 128 * i], ones[0:1, 0:128],
                        abrow_t[0:1, 128 * i:128 * (i + 1)], start=True, stop=True),
                        reads=(ones, abrow_t), writes=(pb[4],))
                sb.op("vector", lambda v: v.tensor_copy(
                    out=bc_t[:, 0:256], in_=pb[4][:, 256:512]),
                    reads=(pb[4],), writes=(bc_t,))
            return emit_ln

        def emit_ln_group(srct, pbt0, pbt1, pbb0, pbb1, bst, mv, abt, abrow_t, bc_t,
                          n_sub=None):
            if n_sub is None:
                n_sub = SUB
            for t_ in range(n_sub):
                for k in range(KT):
                    bank = pbt0 if k < 4 else pbt1
                    sb.op("tensor", lambda t, t_=t_, k=k, bank=bank: t.transpose(
                        bank[:, 128 * (k % 4):128 * (k % 4 + 1)],
                        srct[:, GRP * k + 128 * t_:GRP * k + 128 * (t_ + 1)],
                        ident[:]), reads=(srct, ident), writes=(bank,),
                        inc=(k % 4 == 3))
                sb.op("vector", lambda v, t_=t_: v.bn_stats(
                    out=bst[:, 12 * t_:12 * t_ + 6], in_=pbt0[:, 0:512]),
                    reads=(pbt0,), writes=(bst,))
                sb.op("vector", lambda v, t_=t_: v.bn_stats(
                    out=bst[:, 12 * t_ + 6:12 * t_ + 12], in_=pbt1[:, 0:512]),
                    reads=(pbt1,), writes=(bst,))
                sb.op("vector", lambda v, t_=t_: v.bn_aggr(
                    out=mv[:, 2 * t_:2 * t_ + 2], in_=bst[:, 12 * t_:12 * t_ + 12]),
                    reads=(bst,), writes=(mv,))
            mve = mv[:].rearrange("p (t c) -> p t c", c=2)
            abte = abt[:].rearrange("p (a b) -> p a b", b=1)
            sb.op("vector", lambda v, n_sub=n_sub: v.tensor_scalar(
                out=abte[:, 8:8 + n_sub, :], in0=mve[:, 0:n_sub, 1:2], scalar1=1e-5,
                scalar2=None, op0=ALU.add), reads=(mv,), writes=(abt,))
            sb.op("scalar", lambda s, n_sub=n_sub: s.activation(
                out=abt[:, 8:8 + n_sub], in_=abt[:, 8:8 + n_sub], func=AF.Sqrt),
                reads=(abt,), writes=(abt,))
            sb.op("vector", lambda v, n_sub=n_sub: v.reciprocal(
                out=abt[:, 0:n_sub], in_=abt[:, 8:8 + n_sub]), reads=(abt,), writes=(abt,))
            sb.op("vector", lambda v, n_sub=n_sub: v.tensor_scalar(
                out=abte[:, 12:12 + n_sub, :], in0=mve[:, 0:n_sub, 0:1], scalar1=-1.0,
                scalar2=None, op0=ALU.mult), reads=(mv,), writes=(abt,))
            sb.op("vector", lambda v, n_sub=n_sub: v.tensor_tensor(
                out=abt[:, 4:4 + n_sub], in0=abt[:, 12:12 + n_sub], in1=abt[:, 0:n_sub],
                op=ALU.mult), reads=(abt,), writes=(abt,))
            idxs = list(range(n_sub)) + [4 + i for i in range(n_sub)]
            for j, i in enumerate(idxs):
                bank = pbt0 if j < 4 else pbt1
                sb.op("tensor", lambda t, j=j, i=i, bank=bank: t.transpose(
                    bank[0:1, 128 * (j % 4):128 * (j % 4) + 128], abt[:, i:i + 1],
                    ident[:]), reads=(abt, ident), writes=(bank,),
                    inc=(j == len(idxs) - 1))
            sb.op("vector", lambda v, n_sub=n_sub: v.tensor_copy(
                out=abrow_t[0:1, 0:min(512, 256 * n_sub)],
                in_=pbt0[0:1, 0:min(512, 256 * n_sub)]),
                reads=(pbt0,), writes=(abrow_t,))
            if 2 * n_sub > 4:
                sb.op("vector", lambda v, n_sub=n_sub: v.tensor_copy(
                    out=abrow_t[0:1, 512:256 * n_sub + 256 * n_sub - 512 + 512]
                    if False else abrow_t[0:1, 512:128 * 2 * n_sub],
                    in_=pbt1[0:1, 0:128 * 2 * n_sub - 512]),
                    reads=(pbt1,), writes=(abrow_t,))
            for j in range(2 * n_sub):
                bank = pbb0 if j < n_sub else pbb1
                sb.op("tensor", lambda t, j=j, bank=bank, n_sub=n_sub: t.matmul(
                    bank[:, 128 * (j % n_sub):128 * (j % n_sub + 1)], ones[0:1, 0:128],
                    abrow_t[0:1, 128 * j:128 * (j + 1)], start=True, stop=True),
                    reads=(ones, abrow_t), writes=(bank,),
                    inc=(j % n_sub == n_sub - 1))
            sb.op("vector", lambda v: v.tensor_copy(
                out=bc_t[:, 0:GRP], in_=pbb0[:, 0:GRP]), reads=(pbb0,), writes=(bc_t,))
            sb.op("vector", lambda v: v.tensor_copy(
                out=bc_t[:, GRP:2 * GRP], in_=pbb1[:, 0:GRP]),
                reads=(pbb1,), writes=(bc_t,))

        with contextlib.ExitStack() as actx:
            A = actx.enter_context
            qkvT_s = A(nc.sbuf_tensor("qkvT_s", [128, KT * 3 * D], F32))
            qkvb_s = A(nc.sbuf_tensor("qkvb_s", [128, 16], F32))
            vbias_s = A(nc.sbuf_tensor("vbias_s", [128, D], F32))
            xg = [A(nc.sbuf_tensor("xg%d" % i, [128, KT * GRP], F32)) for i in range(2)]
            xng = A(nc.sbuf_tensor("xng", [128, KT * GRP], F32))
            qkg = A(nc.sbuf_tensor("qkg", [128, 16 * GRP], F32))
            vtg = A(nc.sbuf_tensor("vtg", [128, SUB * 1040], F32))
            bstA = A(nc.sbuf_tensor("bstA", [128, 48], F32))
            mvA = A(nc.sbuf_tensor("mvA", [128, 8], F32))
            abtA = A(nc.sbuf_tensor("abtA", [128, 16], F32))
            abrowA = A(nc.sbuf_tensor("abrowA", [1, 1024], F32))
            bcA = A(nc.sbuf_tensor("bcA", [128, 1024], F32))

            sb.dma("sync", qkvT_s[:].rearrange("p (t c) -> p t c", c=3 * D),
                   dqkvT.rearrange("(t p) c -> p t c", p=128), writes=(qkvT_s,))
            sb.dma("sync", qkvb_s[:], dqkvb, writes=(qkvb_s,))
            sb.dma("sync", vbias_s[:], dvbias, writes=(vbias_s,))
            sb.op("vector", lambda v: v.memset(
                vtg[:].rearrange("p (h c) -> p h c", c=65)[:, :, 64:65], 1.0),
                writes=(vtg,))

            def load_xg(gi):
                sb.dma("sync", xg[gi % 2][:].rearrange("p (t n) -> p t n", n=GRP),
                       dx.rearrange("(t p) n -> p t n", p=128)[:, :, gi * GRP:(gi + 1) * GRP],
                       writes=(xg[gi % 2],))

            if 'A' in phases:
                load_xg(0)
            for g in range(NG if 'A' in phases else 0):
                xgc = xg[g % 2]
                if g + 1 < NG:
                    load_xg(g + 1)
                emit_ln_group(xgc, pb[4], pb[5], pb[6], pb[7], bstA, mvA,
                              abtA, abrowA, bcA)
                for k in range(KT):
                    sb.op("vector", lambda v, k=k, xgc=xgc: v.tensor_tensor(
                        out=xng[:, k * GRP:(k + 1) * GRP],
                        in0=xgc[:, k * GRP:(k + 1) * GRP],
                        in1=bcA[:, 0:GRP], op=ALU.mult),
                        reads=(xgc, bcA), writes=(xng,))
                    sb.op("vector", lambda v, k=k: v.tensor_tensor(
                        out=xng[:, k * GRP:(k + 1) * GRP],
                        in0=xng[:, k * GRP:(k + 1) * GRP],
                        in1=bcA[:, GRP:2 * GRP], op=ALU.add),
                        reads=(xng, bcA), writes=(xng,))
                for ot in range(16):
                    bank = pb[ot % 2]
                    for k in range(KT):
                        sb.op("tensor", lambda t, ot=ot, k=k, bank=bank: t.matmul(
                            bank[:, 0:GRP],
                            qkvT_s[:, (3 * D) * k + 128 * ot:(3 * D) * k + 128 * (ot + 1)],
                            xng[:, k * GRP:(k + 1) * GRP],
                            start=(k == 0), stop=(k == KT - 1)),
                            reads=(qkvT_s, xng), writes=(bank,), inc=(k == KT - 1))
                    sb.op("scalar", lambda s, ot=ot, bank=bank: s.activation(
                        out=qkg[:, ot * GRP:(ot + 1) * GRP], in_=bank[:, 0:GRP],
                        func=AF.Relu, bias=qkvb_s[:, ot:ot + 1], scale=1.0),
                        reads=(bank, qkvb_s), writes=(qkg,))
                sb.op("vector", lambda v: v.tensor_scalar(
                    out=qkg[:], in0=qkg[:], scalar1=1e-6, scalar2=None, op0=ALU.add),
                    reads=(qkg,), writes=(qkg,))
                sb.dma("gpsimd",
                       dqk.rearrange("(t p) n -> p t n", p=128)[:, :, g * GRP:(g + 1) * GRP],
                       qkg[:].rearrange("p (t n) -> p t n", n=GRP),
                       reads=(qkg,), writes=(dqk,))
                for t_ in range(SUB):
                    for half in range(2):
                        bank = pb[2 + half]
                        for k in range(KT):
                            sb.op("tensor", lambda t, t_=t_, half=half, k=k, bank=bank: t.matmul(
                                bank[:, 0:512],
                                xng[:, GRP * k + 128 * t_:GRP * k + 128 * (t_ + 1)],
                                qkvT_s[:, (3 * D) * k + 2 * D + 512 * half:
                                       (3 * D) * k + 2 * D + 512 * (half + 1)],
                                start=(k == 0), stop=(k == KT - 1)),
                                reads=(xng, qkvT_s), writes=(bank,), inc=(k == KT - 1))
                        sb.op("vector", lambda v, t_=t_, half=half, bank=bank: v.tensor_tensor(
                            out=vtg[:].rearrange("p (h c) -> p h c", c=65)[
                                :, 16 * t_ + 8 * half:16 * t_ + 8 * half + 8, 0:64],
                            in0=bank[:, 0:512].rearrange("p (h c) -> p h c", c=64),
                            in1=vbias_s[:, 512 * half:512 * (half + 1)].rearrange(
                                "p (h c) -> p h c", c=64),
                            op=ALU.add), reads=(bank, vbias_s), writes=(vtg,))
                sb.dma("gpsimd",
                       dvt.rearrange("(c p) n -> p c n", p=128)[:, SUB * g:SUB * g + SUB, :],
                       vtg[:].rearrange("p (c n) -> p c n", n=1040),
                       reads=(vtg,), writes=(dvt,))
            sb.barrier()

        with contextlib.ExitStack() as actx:
            A = actx.enter_context
            outT_s = A(nc.sbuf_tensor("outT_s", [128, KT * D], F32))
            outb_s = A(nc.sbuf_tensor("outb_s", [128, 8], F32))
            mask4 = A(nc.sbuf_tensor("mask4", [128, 512], F32))
            qkc = [A(nc.sbuf_tensor("qkc%d" % i, [128, 16 * CP], F32)) for i in range(2)]
            vtc = [A(nc.sbuf_tensor("vtc%d" % i, [128, 1040], F32)) for i in range(2)]
            xgr = [A(nc.sbuf_tensor("xgr%d" % i, [128, KT * GRP], F32)) for i in range(2)]
            ktm = A(nc.sbuf_tensor("ktm", [128, D], F32))
            smb = A(nc.sbuf_tensor("smb", [128, 512], F32))
            kvext = A(nc.sbuf_tensor("kvext", [128, 520], F32))
            aotm = A(nc.sbuf_tensor("aotm", [128, KT * CP], F32))
            ao4 = A(nc.sbuf_tensor("ao4", [128, KT * GRP], F32))
            po4 = A(nc.sbuf_tensor("po4", [128, KT * GRP], F32))
            den4 = A(nc.sbuf_tensor("den4", [128, 16], F32))
            bstA = A(nc.sbuf_tensor("bstA1", [128, 48], F32))
            mvA = A(nc.sbuf_tensor("mvA1", [128, 8], F32))
            abtA = A(nc.sbuf_tensor("abtA1", [128, 16], F32))
            abrowA = A(nc.sbuf_tensor("abrowA1", [1, 1024], F32))
            bcA = A(nc.sbuf_tensor("bcA1", [128, 1024], F32))
            bcQ4 = A(nc.sbuf_tensor("bcQ4", [128, 1024], F32))
            xn3g = A(nc.sbuf_tensor("xn3g", [128, KT * GRP], F32))
            xnqg = A(nc.sbuf_tensor("xnqg", [128, KT * GRP], BF16))

            sb.dma("sync", outT_s[:].rearrange("p (t c) -> p t c", c=D),
                   doutT.rearrange("(t p) c -> p t c", p=128), writes=(outT_s,))
            sb.dma("sync", outb_s[:], doutb, writes=(outb_s,))
            sb.dma("sync", mask4[:], dmask, writes=(mask4,))
            sb.op("vector", lambda v: v.memset(kvext[:], 0.0), writes=(kvext,))

            def load_chunk(ci):
                sb.dma("sync", qkc[ci % 2][:].rearrange("p (t n) -> p t n", n=CP),
                       dqk.rearrange("(t p) n -> p t n", p=128)[:, :, ci * CP:(ci + 1) * CP],
                       reads=(dqk,), writes=(qkc[ci % 2],))
                sb.dma("sync", vtc[ci % 2][:],
                       dvt.rearrange("(c p) n -> p c n", p=128)[:, ci, :],
                       reads=(dvt,), writes=(vtc[ci % 2],))

            def load_xgr(gi):
                sb.dma("sync", xgr[gi % 2][:].rearrange("p (t n) -> p t n", n=GRP),
                       dx.rearrange("(t p) n -> p t n", p=128)[:, :, gi * GRP:(gi + 1) * GRP],
                       writes=(xgr[gi % 2],))

            wgen = make_w_stream(A) if "W" in phases else None
            wdriven = [0]

            def wdrive(n):
                if wgen is not None:
                    for _ in range(n):
                        if next(wgen, StopIteration) is StopIteration:
                            break
                        wdriven[0] += 1

            def wpump(vb):
                if wgen is not None:
                    while vb > 0:
                        tag = next(wgen, None)
                        if tag is None:
                            return
                        wdriven[0] += 1
                        if tag == "v":
                            vb -= 1

            def wdrive_to(k):
                wdrive(max(0, k - wdriven[0]))

            if 'A' in phases:
                load_chunk(0)
                load_xgr(0)
            for g in range(NG if 'A' in phases else 0):
                xgc = xgr[g % 2]
                if g + 1 < NG:
                    load_xgr(g + 1)
                for cc in range(SUB):
                    ci = SUB * g + cc
                    qkfm = qkc[ci % 2]
                    vtm = vtc[ci % 2]
                    if ci + 1 < NCH:
                        load_chunk(ci + 1)
                    wpump(1)
                    for t8 in range(KT):
                        bank = pb[2 + t8 // 4]
                        sb.op("tensor", lambda t, t8=t8, bank=bank, qkfm=qkfm: t.transpose(
                            bank[:, 128 * (t8 % 4):128 * (t8 % 4 + 1)],
                            qkfm[:, (8 + t8) * CP:(9 + t8) * CP], ident[:]),
                            reads=(qkfm, ident), writes=(bank,), inc=(t8 % 4 == 3))
                    for gi in range(2):
                        sb.op("scalar", lambda s, gi=gi: s.copy(
                            out=ktm[:, 512 * gi:512 * (gi + 1)], in_=pb[2 + gi][:, 0:512]),
                            reads=(pb[2 + gi],), writes=(ktm,))
                    for g4 in range(4):
                        bkA = pb[5] if g4 % 2 == 0 else pb[2]
                        bkB = pb[6] if g4 % 2 == 0 else pb[3]
                        numb = pb[6] if g4 % 2 == 0 else pb[4]
                        for hh in range(4):
                            h = 4 * g4 + hh
                            off = 64 * (h % 2)
                            qt = h // 2
                            bank = bkA if h % 2 == 0 else bkB
                            sl = 128 * (hh // 2)
                            sb.op("tensor", lambda t, off=off, qt=qt, bank=bank, sl=sl, qkfm=qkfm: t.matmul(
                                bank[:, sl:sl + 128],
                                qkfm[off:off + 64, (8 + qt) * CP:(9 + qt) * CP],
                                qkfm[off:off + 64, qt * CP:(qt + 1) * CP],
                                start=True, stop=True),
                                reads=(qkfm,), writes=(bank,), inc=(hh == 3))
                        sb.op("vector", lambda v, bkA=bkA: v.tensor_tensor(
                            out=smb[:, 0:256], in0=bkA[:, 0:256], in1=mask4[:, 0:256],
                            op=ALU.mult), reads=(bkA, mask4), writes=(smb,))
                        sb.op("vector", lambda v, bkB=bkB: v.tensor_tensor(
                            out=smb[:, 256:512], in0=bkB[:, 0:256], in1=mask4[:, 0:256],
                            op=ALU.mult), reads=(bkB, mask4), writes=(smb,))
                        for hh in range(4):
                            h = 4 * g4 + hh
                            off = 64 * (h % 2)
                            gp = h // 2
                            if gp < 7:
                                outap = pb[7][off:off + 64, 65 * gp:65 * (gp + 1)]
                                wbank = pb[7]
                            else:
                                outap = pb[5][off:off + 64, 384:449]
                                wbank = pb[5]
                            sb.op("tensor", lambda t, h=h, outap=outap, vtm=vtm: t.matmul(
                                outap, ktm[:, 64 * h:64 * (h + 1)],
                                vtm[:, 65 * h:65 * (h + 1)],
                                start=True, stop=True),
                                reads=(ktm, vtm), writes=(wbank,), inc=(hh == 3))
                        for hh in range(4):
                            h = 4 * g4 + hh
                            off = 64 * (h % 2)
                            gp = h // 2
                            ssl = 256 * (h % 2) + 128 * (hh // 2)
                            sb.op("tensor", lambda t, hh=hh, h=h, ssl=ssl, vtm=vtm, numb=numb: t.matmul(
                                numb[:, 128 * hh:128 * hh + 65],
                                smb[:, ssl:ssl + 128],
                                vtm[:, 65 * h:65 * (h + 1)], start=True, stop=False),
                                reads=(smb, vtm), writes=(numb,), inc=False)
                            sb.op("tensor", lambda t, hh=hh, h=h, off=off, gp=gp, qkfm=qkfm, numb=numb: t.matmul(
                                numb[:, 128 * hh:128 * hh + 65],
                                qkfm[off:off + 64, (h // 2) * CP:(h // 2 + 1) * CP],
                                kvext[off:off + 64, 65 * gp:65 * (gp + 1)],
                                start=False, stop=True),
                                reads=(qkfm, kvext), writes=(numb,), inc=(hh == 3))
                        sb.op("vector", lambda v, g4=g4, numb=numb: v.tensor_scalar(
                            out=den4[:, 4 * g4:4 * g4 + 4],
                            in0=numb[:, 0:512].rearrange("p (s c) -> p s c", c=128)[:, :, 64:65],
                            scalar1=1e-6, scalar2=None, op0=ALU.max),
                            reads=(numb,), writes=(den4,))
                        sb.op("vector", lambda v, g4=g4: v.reciprocal(
                            out=den4[:, 4 * g4:4 * g4 + 4], in_=den4[:, 4 * g4:4 * g4 + 4]),
                            reads=(den4,), writes=(den4,))
                        for hh in range(4):
                            h = 4 * g4 + hh
                            dst = 128 * (h // 2) + 64 * (h % 2)
                            sb.op("scalar", lambda s, hh=hh, dst=dst, g4=g4, numb=numb: s.activation(
                                out=aotm[:, dst:dst + 64],
                                in_=numb[:, 128 * hh:128 * hh + 64],
                                func=AF.Identity,
                                scale=den4[:, 4 * g4 + hh:4 * g4 + hh + 1]),
                                reads=(numb, den4), writes=(aotm,))
                        wpump(1)
                    sb.op("vector", lambda v: v.tensor_tensor(
                        out=kvext[:, 0:455], in0=kvext[:, 0:455], in1=pb[7][:, 0:455],
                        op=ALU.add), reads=(kvext, pb[7]), writes=(kvext,))
                    sb.op("vector", lambda v: v.tensor_tensor(
                        out=kvext[:, 455:520], in0=kvext[:, 455:520],
                        in1=pb[5][:, 384:449],
                        op=ALU.add), reads=(kvext, pb[5]), writes=(kvext,))
                    for pr in range(8):
                        bank = pb[2 + pr // 4]
                        sb.op("tensor", lambda t, pr=pr, bank=bank: t.transpose(
                            bank[:, 128 * (pr % 4):128 * (pr % 4 + 1)],
                            aotm[:, 128 * pr:128 * (pr + 1)], ident[:]),
                            reads=(aotm, ident), writes=(bank,), inc=(pr % 4 == 3))
                    for gi in range(2):
                        sb.op("scalar", lambda s, gi=gi, cc=cc: s.copy(
                            out=ao4[:].rearrange("p (t n) -> p t n", n=GRP)[
                                :, 4 * gi:4 * gi + 4, 128 * cc:128 * (cc + 1)],
                            in_=pb[2 + gi][:, 0:512].rearrange("p (t n) -> p t n", n=128)),
                            reads=(pb[2 + gi],), writes=(ao4,))
                for ot in range(8):
                    bank = pb[ot % 2]
                    for k in range(KT):
                        sb.op("tensor", lambda t, ot=ot, k=k, bank=bank: t.matmul(
                            bank[:, 0:GRP],
                            outT_s[:, D * k + 128 * ot:D * k + 128 * (ot + 1)],
                            ao4[:, k * GRP:(k + 1) * GRP],
                            start=(k == 0), stop=(k == KT - 1)),
                            reads=(outT_s, ao4), writes=(bank,), inc=(k == KT - 1))
                    sb.op("scalar", lambda s, ot=ot, bank=bank: s.activation(
                        out=po4[:, ot * GRP:(ot + 1) * GRP], in_=bank[:, 0:GRP],
                        func=AF.Identity, bias=outb_s[:, ot:ot + 1], scale=1.0),
                        reads=(bank, outb_s), writes=(po4,))
                wpump(10)
                emit_ln_group(po4, pb[4], pb[5], pb[6], pb[7], bstA, mvA,
                              abtA, abrowA, bcA)
                for k in range(KT):
                    sb.op("vector", lambda v, k=k: v.tensor_tensor(
                        out=po4[:, k * GRP:(k + 1) * GRP],
                        in0=po4[:, k * GRP:(k + 1) * GRP],
                        in1=bcA[:, 0:GRP], op=ALU.mult),
                        reads=(po4, bcA), writes=(po4,))
                    sb.op("vector", lambda v, k=k: v.tensor_tensor(
                        out=po4[:, k * GRP:(k + 1) * GRP],
                        in0=po4[:, k * GRP:(k + 1) * GRP],
                        in1=bcA[:, GRP:2 * GRP], op=ALU.add),
                        reads=(po4, bcA), writes=(po4,))
                    sb.op("vector", lambda v, k=k, xgc=xgc: v.tensor_tensor(
                        out=po4[:, k * GRP:(k + 1) * GRP],
                        in0=po4[:, k * GRP:(k + 1) * GRP],
                        in1=xgc[:, k * GRP:(k + 1) * GRP], op=ALU.add),
                        reads=(po4, xgc), writes=(po4,))
                sb.dma("gpsimd",
                       dx2.rearrange("(t p) n -> p t n", p=128)[:, :, g * GRP:(g + 1) * GRP],
                       po4[:].rearrange("p (t n) -> p t n", n=GRP),
                       reads=(po4,), writes=(dx2,))
                wdrive_to(33)
                emit_ln_group(po4, pb[4], pb[5], pb[6], pb[7], bstA, mvA,
                              abtA, abrowA, bcA)
                for k in range(KT):
                    sb.op("vector", lambda v, k=k: v.tensor_tensor(
                        out=xn3g[:, k * GRP:(k + 1) * GRP],
                        in0=po4[:, k * GRP:(k + 1) * GRP],
                        in1=bcA[:, 0:GRP], op=ALU.mult),
                        reads=(po4, bcA), writes=(xn3g,))
                    sb.op("vector", lambda v, k=k: v.tensor_tensor(
                        out=xn3g[:, k * GRP:(k + 1) * GRP],
                        in0=xn3g[:, k * GRP:(k + 1) * GRP],
                        in1=bcA[:, GRP:2 * GRP], op=ALU.add),
                        reads=(xn3g, bcA), writes=(xn3g,))
                sb.op("scalar", lambda s: s.activation(
                    out=ao4[:], in_=xn3g[:], func=AF.Abs), reads=(xn3g,), writes=(ao4,))
                for lv in (4, 2, 1):
                    sb.op("vector", lambda v, lv=lv: v.tensor_tensor(
                        out=ao4[:, 0:lv * GRP], in0=ao4[:, 0:lv * GRP],
                        in1=ao4[:, lv * GRP:2 * lv * GRP], op=ALU.max),
                        reads=(ao4,), writes=(ao4,))
                for t_ in range(SUB):
                    sb.op("tensor", lambda t, t_=t_: t.transpose(
                        pb[4][:, 128 * t_:128 * (t_ + 1)],
                        ao4[:, 128 * t_:128 * (t_ + 1)], ident[:]),
                        reads=(ao4, ident), writes=(pb[4],), inc=(t_ == SUB - 1))
                for t_ in range(SUB):
                    sb.op("vector", lambda v, t_=t_: v.tensor_reduce(
                        out=abtA[:].rearrange("p (a b) -> p a b", b=1)[:, 8 + t_, :],
                        in_=pb[4][:, 128 * t_:128 * (t_ + 1)], axis=AX.X, op=ALU.max),
                        reads=(pb[4],), writes=(abtA,))
                sb.op("vector", lambda v: v.tensor_scalar(
                    out=abtA[:, 8:8 + SUB], in0=abtA[:, 8:8 + SUB], scalar1=1e-5,
                    scalar2=None, op0=ALU.max), reads=(abtA,), writes=(abtA,))
                sb.op("vector", lambda v: v.reciprocal(
                    out=abtA[:, 0:SUB], in_=abtA[:, 8:8 + SUB]),
                    reads=(abtA,), writes=(abtA,))
                sb.op("vector", lambda v: v.tensor_scalar(
                    out=abtA[:, 0:SUB], in0=abtA[:, 0:SUB], scalar1=127.0,
                    scalar2=None, op0=ALU.mult), reads=(abtA,), writes=(abtA,))
                sb.op("vector", lambda v: v.tensor_scalar(
                    out=abtA[:, 4:4 + SUB], in0=abtA[:, 8:8 + SUB],
                    scalar1=pers[:, 1:2], scalar2=None, op0=ALU.mult),
                    reads=(abtA, pers), writes=(abtA,))
                idxs = list(range(SUB)) + [4 + i for i in range(SUB)]
                for j, i in enumerate(idxs):
                    bank = pb[4] if j < 4 else pb[5]
                    sb.op("tensor", lambda t, j=j, i=i, bank=bank: t.transpose(
                        bank[0:1, 128 * (j % 4):128 * (j % 4) + 128], abtA[:, i:i + 1],
                        ident[:]), reads=(abtA, ident), writes=(bank,),
                        inc=(j == len(idxs) - 1))
                sb.op("vector", lambda v: v.tensor_copy(
                    out=abrowA[0:1, 0:min(512, 256 * SUB)],
                    in_=pb[4][0:1, 0:min(512, 256 * SUB)]),
                    reads=(pb[4],), writes=(abrowA,))
                if 2 * SUB > 4:
                    sb.op("vector", lambda v: v.tensor_copy(
                        out=abrowA[0:1, 512:128 * 2 * SUB],
                        in_=pb[5][0:1, 0:128 * 2 * SUB - 512]),
                        reads=(pb[5],), writes=(abrowA,))
                for j in range(2 * SUB):
                    bank = pb[6] if j < SUB else pb[7]
                    sb.op("tensor", lambda t, j=j, bank=bank: t.matmul(
                        bank[:, 128 * (j % SUB):128 * (j % SUB + 1)], ones[0:1, 0:128],
                        abrowA[0:1, 128 * j:128 * (j + 1)], start=True, stop=True),
                        reads=(ones, abrowA), writes=(bank,), inc=(j % SUB == SUB - 1))
                sb.op("vector", lambda v: v.tensor_copy(
                    out=bcQ4[:, 0:GRP], in_=pb[6][:, 0:GRP]),
                    reads=(pb[6],), writes=(bcQ4,))
                sb.op("vector", lambda v: v.tensor_copy(
                    out=bcQ4[:, GRP:2 * GRP], in_=pb[7][:, 0:GRP]),
                    reads=(pb[7],), writes=(bcQ4,))
                for k in range(KT):
                    sb.op("vector", lambda v, k=k: v.tensor_tensor(
                        out=xn3g[:, k * GRP:(k + 1) * GRP],
                        in0=xn3g[:, k * GRP:(k + 1) * GRP],
                        in1=bcQ4[:, 0:GRP], op=ALU.mult),
                        reads=(xn3g, bcQ4), writes=(xn3g,))
                sb.op("vector", lambda v: v.tensor_scalar(
                    out=ao4[:], in0=xn3g[:], scalar1=C32, scalar2=C32,
                    op0=ALU.add, op1=ALU.subtract), reads=(xn3g,), writes=(ao4,))
                sb.op("vector", lambda v: v.tensor_scalar(
                    out=xn3g[:], in0=xn3g[:], scalar1=C52, scalar2=C52,
                    op0=ALU.add, op1=ALU.subtract), reads=(xn3g,), writes=(xn3g,))
                sb.op("vector", lambda v: v.scalar_tensor_tensor(
                    out=xnqg[:], in0=ao4[:], scalar=flagg[:, 1:2], in1=xn3g[:],
                    op0=ALU.mult, op1=ALU.add),
                    reads=(ao4, xn3g, flagg), writes=(xnqg,))
                sb.dma("gpsimd",
                       dxnq.rearrange("(t p) n -> p t n", p=128)[:, :, g * GRP:(g + 1) * GRP],
                       xnqg[:].rearrange("p (t n) -> p t n", n=GRP),
                       reads=(xnqg,), writes=(dxnq,))
                sb.dma("gpsimd", dinv[:, g * GRP:(g + 1) * GRP],
                       bcQ4[:, GRP:2 * GRP], reads=(bcQ4,), writes=(dinv,))
                wpump(2)
                if dbg and g == NG - 1:
                    for nm, ten in (("qkfm", qkc[(SUB * g + SUB - 1) % 2]),
                                    ("vtm", vtc[(SUB * g + SUB - 1) % 2]),
                                    ("kvext", kvext), ("x2c", po4)):
                        sb.dma("gpsimd", ddbg[nm], ten[:], reads=(ten,))
            wdrive(10 ** 6)
            sb.barrier()

        with contextlib.ExitStack() as bctx:
            B_ = bctx.enter_context

            w1q_s = B_(nc.sbuf_tensor("w1q_s", [128, KT * DH], FP8))
            w2q_s = B_(nc.sbuf_tensor("w2q_s", [128, HT * D], FP8))
            cw_s = [B_(nc.sbuf_tensor("cw%d_s" % t, [128, HT], F32)) for t in range(3)]
            cb_s = B_(nc.sbuf_tensor("cb_s", [128, HT], F32))
            x2b = [B_(nc.sbuf_tensor("x2b%d" % i, [128, KT * CP], F32)) for i in range(4)]
            fv = B_(nc.sbuf_tensor("fv", [128, KT * CP], F32))
            xnqc = [B_(nc.sbuf_tensor("xnqc%d" % i, [128, KT * CP], BF16)) for i in range(2)]
            invc4 = [B_(nc.sbuf_tensor("invc4%d" % i, [128, 512], F32)) for i in range(2)]
            hbuf = [B_(nc.sbuf_tensor("hbuf%d" % i, [128, HT * (CP + 2)], F32))
                    for i in range(3)]
            hs = B_(nc.sbuf_tensor("hs", [128, HT * CP], F32))
            rs = B_(nc.sbuf_tensor("rs", [128, HT * CP], F32))
            hq = [B_(nc.sbuf_tensor("hq%d" % i, [128, HT * CP], BF16))
                  for i in range(2)]
            bstT = B_(nc.sbuf_tensor("bstT", [128, 12], F32))
            mvT = B_(nc.sbuf_tensor("mvT", [128, 2], F32))
            abtT = B_(nc.sbuf_tensor("abtT", [128, 8], F32))
            abrowT = B_(nc.sbuf_tensor("abrowT", [1, 256], F32))
            bcT = B_(nc.sbuf_tensor("bcT", [128, 1024], F32))
            bcTQ = [B_(nc.sbuf_tensor("bcTQ%d" % i, [128, 1024], F32))
                    for i in range(2)]
            abtQ = B_(nc.sbuf_tensor("abtQ", [128, 8], F32))
            tcol = B_(nc.sbuf_tensor("tcol", [128, 32], F32))
            abrowQ = B_(nc.sbuf_tensor("abrowQ", [1, 256], F32))

            sb.dma("sync", w1q_s[:].rearrange("p (t c) -> p t c", c=DH),
                   dw1q.rearrange("(t p) c -> p t c", p=128), reads=(dw1q,),
                   writes=(w1q_s,))
            sb.dma("sync", w2q_s[:].rearrange("p (t c) -> p t c", c=D),
                   dw2q.rearrange("(t p) c -> p t c", p=128), reads=(dw2q,),
                   writes=(w2q_s,))
            for t3 in range(3):
                sb.dma("sync", cw_s[t3][:], dcw[t3], writes=(cw_s[t3],))
            sb.dma("sync", cb_s[:], dcb, writes=(cb_s,))

            def ln_chunk(srct, pbA, pbB, bst, mv, abt, abrow_t, bc_t):
                for k in range(KT):
                    bank = pbA if k < 4 else pbB
                    sb.op("tensor", lambda t, k=k, bank=bank: t.transpose(
                        bank[:, 128 * (k % 4):128 * (k % 4 + 1)],
                        srct[:, k * CP:(k + 1) * CP], ident[:]),
                        reads=(srct, ident), writes=(bank,), inc=(k % 4 == 3))
                yield
                sb.op("vector", lambda v: v.bn_stats(
                    out=bst[:, 0:6], in_=pbA[:, 0:512]), reads=(pbA,), writes=(bst,))
                sb.op("vector", lambda v: v.bn_stats(
                    out=bst[:, 6:12], in_=pbB[:, 0:512]), reads=(pbB,), writes=(bst,))
                sb.op("vector", lambda v: v.bn_aggr(
                    out=mv[:, 0:2], in_=bst[:, 0:12]), reads=(bst,), writes=(mv,))
                sb.op("vector", lambda v: v.tensor_scalar(
                    out=abt[:, 2:3], in0=mv[:, 1:2], scalar1=1e-5, scalar2=None,
                    op0=ALU.add), reads=(mv,), writes=(abt,))
                sb.op("scalar", lambda s: s.activation(
                    out=abt[:, 3:4], in_=abt[:, 2:3], func=AF.Sqrt),
                    reads=(abt,), writes=(abt,))
                sb.op("vector", lambda v: v.reciprocal(
                    out=abt[:, 0:1], in_=abt[:, 3:4]), reads=(abt,), writes=(abt,))
                sb.op("vector", lambda v: v.tensor_scalar(
                    out=abt[:, 4:5], in0=mv[:, 0:1], scalar1=-1.0, scalar2=None,
                    op0=ALU.mult), reads=(mv,), writes=(abt,))
                sb.op("vector", lambda v: v.tensor_tensor(
                    out=abt[:, 1:2], in0=abt[:, 4:5], in1=abt[:, 0:1],
                    op=ALU.mult), reads=(abt,), writes=(abt,))
                yield
                bc2_pair(abt, 0, 1, abrow_t, bc_t, pbA)

            def bc2_pair(abt, c0, c1, abrow_t, bc_t, pbX):
                for i, cc in enumerate((c0, c1)):
                    sb.op("tensor", lambda t, i=i, cc=cc, abt=abt: t.transpose(
                        pbX[0:1, 16 + 128 * i:144 + 128 * i], abt[:, cc:cc + 1],
                        ident[:]), reads=(abt, ident), writes=(pbX,), inc=(i == 1))
                sb.op("vector", lambda v: v.tensor_copy(
                    out=abrow_t[0:1, 0:256], in_=pbX[0:1, 16:272]),
                    reads=(pbX,), writes=(abrow_t,))
                for i in range(4):
                    sb.op("tensor", lambda t, i=i: t.matmul(
                        pbX[:, 128 * i:128 * (i + 1)], ones[0:1, 0:128],
                        abrow_t[0:1, 128 * (i // 2):128 * (i // 2 + 1)],
                        start=True, stop=True),
                        reads=(ones, abrow_t), writes=(pbX,), inc=(i == 3))
                for j, dst in ((0, 0), (1, 512)):
                    sb.op("vector", lambda v, j=j, dst=dst: v.tensor_copy(
                        out=bc_t[:, dst:dst + 256], in_=pbX[:, 256 * j:256 * (j + 1)]),
                        reads=(pbX,), writes=(bc_t,))
                    sb.op("vector", lambda v, dst=dst: v.tensor_copy(
                        out=bc_t[:, dst + 256:dst + 512], in_=bc_t[:, dst:dst + 256]),
                        reads=(bc_t,), writes=(bc_t,))

            def qscale(maxcol_ap, maxcol_t, pcol, pbX, abt, abrow_t, bc_t):
                sb.op("tensor", lambda t: t.transpose(
                    pbX[:, 384:512], maxcol_ap, ident[:]),
                    reads=(maxcol_t, ident), writes=(pbX,))
                yield
                sb.op("vector", lambda v, abt=abt: v.tensor_reduce(
                    out=abt[:, 5:6], in_=pbX[:, 384:512], axis=AX.X, op=ALU.max),
                    reads=(pbX,), writes=(abt,))
                sb.op("vector", lambda v, abt=abt: v.tensor_scalar(
                    out=abt[:, 5:6], in0=abt[:, 5:6], scalar1=1e-5, scalar2=None,
                    op0=ALU.max), reads=(abt,), writes=(abt,))
                sb.op("vector", lambda v, abt=abt: v.reciprocal(
                    out=abt[:, 6:7], in_=abt[:, 5:6]), reads=(abt,), writes=(abt,))
                sb.op("vector", lambda v, abt=abt: v.tensor_scalar(
                    out=abt[:, 7:8], in0=abt[:, 6:7], scalar1=127.0, scalar2=None,
                    op0=ALU.mult), reads=(abt,), writes=(abt,))
                sb.op("vector", lambda v, abt=abt, pcol=pcol: v.tensor_scalar(
                    out=abt[:, 6:7], in0=abt[:, 5:6], scalar1=pers[:, pcol:pcol + 1],
                    scalar2=None, op0=ALU.mult), reads=(abt, pers), writes=(abt,))
                yield
                bc2_pair(abt, 7, 6, abrow_t, bc_t, pbX)

            hb3 = [hbuf[i][:].rearrange("p (t c) -> p t c", c=CP + 2) for i in range(3)]
            hs3 = hs[:].rearrange("p (t c) -> p t c", c=CP)

            def load_bhead(ci):
                sb.dma("sync", xnqc[ci % 2][:].rearrange("p (t n) -> p t n", n=CP),
                       dxnq.rearrange("(t p) n -> p t n", p=128)[:, :, ci * CP:(ci + 1) * CP],
                       reads=(dxnq,), writes=(xnqc[ci % 2],))
                sb.dma("sync", invc4[ci % 2][:, 0:128], dinv[:, ci * CP:(ci + 1) * CP],
                       reads=(dinv,), writes=(invc4[ci % 2],))

            if 'B' in phases:
                sb.dma("sync", x2b[0][:].rearrange("p (t n) -> p t n", n=CP),
                       dx2.rearrange("(t p) n -> p t n", p=128)[:, :, 0:CP],
                       reads=(dx2,), writes=(x2b[0],))
                load_bhead(0)
            def head_stages(c):
                if True:
                    sb.op("vector", lambda v, c=c: v.tensor_copy(
                        out=invc4[c % 2][:, 128:256], in_=invc4[c % 2][:, 0:128]),
                        reads=(invc4[c % 2],), writes=(invc4[c % 2],))
                    sb.op("vector", lambda v, c=c: v.tensor_copy(
                        out=invc4[c % 2][:, 256:512], in_=invc4[c % 2][:, 0:256]),
                        reads=(invc4[c % 2],), writes=(invc4[c % 2],))
                    yield
                    hb = hbuf[c % 3]
                    for grp in range(8):
                        bank = pb[grp % 2]
                        for sl in range(4):
                            ot = 4 * grp + sl
                            for k in range(KT):
                                sb.op("tensor",
                                      lambda t, ot=ot, sl=sl, k=k, bank=bank, c=c: t.matmul(
                                          bank[:, 128 * sl:128 * sl + CP],
                                          w1q_s[:, DH * k + 128 * ot:DH * k + 128 * (ot + 1)],
                                          xnqc[c % 2][:, k * CP:(k + 1) * CP],
                                          start=(k == 0), stop=(k == KT - 1)),
                                      reads=(w1q_s, xnqc[c % 2]), writes=(bank,),
                                      inc=(sl == 3 and k == KT - 1))
                        sb.op("vector", lambda v, grp=grp, bank=bank, hbi=c % 3, c=c: v.tensor_tensor(
                            out=hb3[hbi][:, 4 * grp:4 * grp + 4, 1:1 + CP],
                            in0=bank[:, 0:512].rearrange("p (s c) -> p s c", c=128),
                            in1=invc4[c % 2][:, 0:512].rearrange("p (s c) -> p s c", c=128),
                            op=ALU.mult), reads=(bank, invc4[c % 2]), writes=(hb,))
                        if grp == 3:
                            yield
                    yield
                    if c == 0:
                        sb.op("vector", lambda v: v.memset(
                            hb3[0][:, :, 0:1], 0.0), writes=(hb,))
                    else:
                        sb.op("vector", lambda v, ci=c: v.tensor_copy(
                            out=hb3[ci % 3][:, :, 0:1],
                            in_=hb3[(ci - 1) % 3][:, :, CP:CP + 1]),
                            reads=(hbuf[(c - 1) % 3],), writes=(hb,))
            def tail_pre_stages(cm):
                hb = hbuf[cm % 3]
                hbv = hb3[cm % 3]
                rs3 = rs[:].rearrange("p (t c) -> p t c", c=CP)
                bcq = bcTQ[cm % 2]
                hqc = hq[cm % 2]
                for t32 in range(HT):
                    sb.op("scalar", lambda s, t=t32: s.activation(
                        out=hs3[:, t:t + 1, :], in_=hbv[:, t:t + 1, 0:CP],
                        func=AF.Identity, scale=cw_s[0][:, t:t + 1],
                        bias=cb_s[:, t:t + 1]),
                        reads=(hb, cw_s[0], cb_s), writes=(hs,))
                    sb.op("vector", lambda v, t=t32: v.scalar_tensor_tensor(
                        out=hs3[:, t:t + 1, :], in0=hbv[:, t:t + 1, 1:1 + CP],
                        scalar=cw_s[1][:, t:t + 1], in1=hs3[:, t:t + 1, :],
                        op0=ALU.mult, op1=ALU.add),
                        reads=(hb, cw_s[1], hs), writes=(hs,))
                    sb.op("scalar", lambda s, t=t32: s.activation(
                        out=rs3[:, t:t + 1, 0:CP - 1], in_=hbv[:, t:t + 1, 2:1 + CP],
                        func=AF.Identity, scale=cw_s[2][:, t:t + 1]),
                        reads=(hb, cw_s[2]), writes=(rs,))
                    if t32 == 15:
                        yield
                sb.op("vector", lambda v: v.memset(
                    rs3[:, :, CP - 1:CP], 0.0), writes=(rs,))
                yield
                for g in range(4):
                    sb.op("vector", lambda v, g=g: v.tensor_tensor(
                        out=hs[:, 1024 * g:1024 * (g + 1)],
                        in0=hs[:, 1024 * g:1024 * (g + 1)],
                        in1=rs[:, 1024 * g:1024 * (g + 1)], op=ALU.add),
                        reads=(hs, rs), writes=(hs,))
                yield
                if cm + 1 < NCH:
                    sb.op("vector", lambda v, cm=cm: v.tensor_copy(
                        out=hb3[cm % 3][:, :, CP + 1:CP + 2],
                        in_=hb3[(cm + 1) % 3][:, :, 1:2]),
                        reads=(hbuf[(cm + 1) % 3],), writes=(hb,))
                else:
                    sb.op("vector", lambda v, cm=cm: v.memset(
                        hb3[cm % 3][:, :, CP + 1:CP + 2], 0.0), writes=(hb,))
                tcv = tcol[:].rearrange("p (t c) -> p t c", c=1)
                cw2v = cw_s[2][:].rearrange("p (t c) -> p t c", c=1)
                sb.op("vector", lambda v: v.tensor_tensor(
                    out=tcv[:, :, :], in0=hbv[:, :, CP + 1:CP + 2], in1=cw2v[:, :, :],
                    op=ALU.mult), reads=(hb, cw_s[2]), writes=(tcol,))
                sb.op("vector", lambda v: v.tensor_tensor(
                    out=hs3[:, :, CP - 1:CP], in0=hs3[:, :, CP - 1:CP],
                    in1=tcv[:, :, :], op=ALU.add), reads=(hs, tcol), writes=(hs,))
                yield
                for g8 in range(4):
                    sb.op("scalar", lambda s, g8=g8: s.activation(
                        out=hs[:, 1024 * g8:1024 * (g8 + 1)],
                        in_=hs[:, 1024 * g8:1024 * (g8 + 1)], func=AF.Silu),
                        reads=(hs,), writes=(hs,))
                sb.op("scalar", lambda s: s.activation(
                    out=rs[:], in_=hs[:], func=AF.Abs), reads=(hs,), writes=(rs,))
                yield
                for span in (2048, 1024, 512, 256, 128):
                    sb.op("vector", lambda v, span=span: v.tensor_tensor(
                        out=rs[:, 0:span], in0=rs[:, 0:span],
                        in1=rs[:, span:2 * span], op=ALU.max),
                        reads=(rs,), writes=(rs,))
                yield
                yield from qscale(rs[:, 0:128], rs, 3, pb[4], abtQ, abrowQ, bcq)
                yield
                for g in range(8):
                    sb.op("vector", lambda v, g=g, bcq=bcq: v.tensor_tensor(
                        out=hs[:, 512 * g:512 * (g + 1)],
                        in0=hs[:, 512 * g:512 * (g + 1)],
                        in1=bcq[:, 0:512], op=ALU.mult),
                        reads=(hs, bcq), writes=(hs,))
                sb.op("vector", lambda v: v.tensor_scalar(
                    out=rs[:], in0=hs[:], scalar1=C32, scalar2=C32,
                    op0=ALU.add, op1=ALU.subtract), reads=(hs,), writes=(rs,))
                sb.op("vector", lambda v: v.tensor_scalar(
                    out=hs[:], in0=hs[:], scalar1=C52, scalar2=C52,
                    op0=ALU.add, op1=ALU.subtract), reads=(hs,), writes=(hs,))
                sb.op("vector", lambda v, hqc=hqc: v.scalar_tensor_tensor(
                    out=hqc[:], in0=rs[:], scalar=flagg[:, 1:2], in1=hs[:],
                    op0=ALU.mult, op1=ALU.add),
                    reads=(rs, hs, flagg), writes=(hqc,))

            def tail_post_stages(cm):
                hb = hbuf[cm % 3]
                bcq = bcTQ[cm % 2]
                hqc = hq[cm % 2]
                for ot in range(8):
                    bank = pb[2 + ot % 2]
                    for k in range(HT):
                        sb.op("tensor", lambda t, ot=ot, k=k, bank=bank, hqc=hqc: t.matmul(
                            bank[:, 0:CP],
                            w2q_s[:, D * k + 128 * ot:D * k + 128 * (ot + 1)],
                            hqc[:, k * CP:(k + 1) * CP],
                            start=(k == 0), stop=(k == HT - 1)),
                            reads=(w2q_s, hqc), writes=(bank,), inc=(k == HT - 1))
                    sb.op("vector", lambda v, ot=ot, bank=bank, bcq=bcq: v.tensor_tensor(
                        out=fv[:, 128 * ot:128 * (ot + 1)], in0=bank[:, 0:CP],
                        in1=bcq[:, 512:640], op=ALU.mult),
                        reads=(bank, bcq), writes=(fv,))
                    if ot == 3:
                        yield
                yield
                yield from ln_chunk(fv, pb[6], pb[7], bstT, mvT, abtT, abrowT, bcT)
                yield
                x2old = x2b[cm % 4]
                for g in range(2):
                    sb.op("vector", lambda v, g=g: v.tensor_tensor(
                        out=fv[:, 512 * g:512 * (g + 1)],
                        in0=fv[:, 512 * g:512 * (g + 1)],
                        in1=bcT[:, 0:512], op=ALU.mult),
                        reads=(fv, bcT), writes=(fv,))
                for g in range(2):
                    sb.op("vector", lambda v, g=g: v.tensor_tensor(
                        out=fv[:, 512 * g:512 * (g + 1)],
                        in0=fv[:, 512 * g:512 * (g + 1)],
                        in1=bcT[:, 512:1024], op=ALU.add),
                        reads=(fv, bcT), writes=(fv,))
                for g in range(2):
                    sb.op("vector", lambda v, g=g, x2old=x2old: v.tensor_tensor(
                        out=fv[:, 512 * g:512 * (g + 1)],
                        in0=fv[:, 512 * g:512 * (g + 1)],
                        in1=x2old[:, 512 * g:512 * (g + 1)], op=ALU.add),
                        reads=(fv, x2old), writes=(fv,))
                sb.dma("gpsimd",
                       dout.rearrange("(t p) n -> p t n", p=128)[:, :, cm * CP:(cm + 1) * CP],
                       fv[:].rearrange("p (t n) -> p t n", n=CP),
                       reads=(fv,), writes=(dout,))
                if dbg and cm == NCH - 1:
                    sb.dma("gpsimd", ddbg["hbuf"], hb[:], reads=(hb,))
                    sb.dma("gpsimd", ddbg["hs"], hs[:], reads=(hs,))
                    sb.dma("gpsimd", ddbg["gam"], pers[:], reads=(pers,))

            for c in range((NCH + 2) if 'B' in phases else 0):
                if c + 1 < NCH:
                    sb.dma("sync", x2b[(c + 1) % 4][:].rearrange("p (t n) -> p t n", n=CP),
                           dx2.rearrange("(t p) n -> p t n", p=128)[
                               :, :, (c + 1) * CP:(c + 2) * CP],
                           reads=(dx2,), writes=(x2b[(c + 1) % 4],))
                    load_bhead(c + 1)
                import os as _os
                post = tail_post_stages(c - 2) if 0 <= c - 2 else None
                pre = tail_pre_stages(c - 1) if 0 <= c - 1 < NCH else None
                if _os.environ.get("NOINTER") == "1":
                    if post is not None:
                        for _ in post:
                            pass
                    if c < NCH:
                        for _ in head_stages(c):
                            pass
                    if pre is not None:
                        for _ in pre:
                            pass
                else:
                    if pre is not None:
                        for _ in range(2):
                            next(pre, None)
                    if post is not None:
                        for _ in range(2):
                            next(post, None)
                    if c < NCH:
                        for _ in head_stages(c):
                            pass
                    gens = [g for g in (pre, post) if g is not None]
                    while gens:
                        for ggen in list(gens):
                            try:
                                next(ggen)
                            except StopIteration:
                                gens.remove(ggen)
            sb.barrier()
    return nc


def host_inputs(qkv_w, qkv_b, out_w, out_b, w1, conv_w, conv_b, w2):
    qkv_w = np.asarray(qkv_w, dtype=np.float32)
    qkv_b = np.asarray(qkv_b, dtype=np.float32)
    out_w = np.asarray(out_w, dtype=np.float32)
    out_b = np.asarray(out_b, dtype=np.float32)
    w1 = np.asarray(w1, dtype=np.float32)
    conv_w = np.asarray(conv_w, dtype=np.float32)
    conv_b = np.asarray(conv_b, dtype=np.float32)
    w2 = np.asarray(w2, dtype=np.float32)
    ins = {}
    ins["qkvT"] = np.ascontiguousarray(qkv_w.T)
    ins["qkvb"] = np.ascontiguousarray(qkv_b[:2048].reshape(16, 128).T)
    ins["vbias_bc"] = np.ascontiguousarray(np.broadcast_to(qkv_b[2048:], (128, 1024)))
    ins["outT"] = np.ascontiguousarray(out_w.T)
    ins["outb"] = np.ascontiguousarray(out_b.reshape(8, 128).T)
    ins["w1T"] = np.ascontiguousarray(w1.T)
    ins["w2T"] = np.ascontiguousarray(w2.T)
    cw = conv_w.reshape(DH, 3)
    for t in range(3):
        ins["convw%d" % t] = np.ascontiguousarray(cw[:, t].reshape(HT, 128).T)
    ins["convb"] = np.ascontiguousarray(conv_b.reshape(HT, 128).T)
    mask1 = np.triu(np.ones((128, 128), dtype=np.float32))
    ins["mask4d"] = np.ascontiguousarray(np.tile(mask1, (1, 4)))
    ins["identd"] = np.eye(128, dtype=np.float32)
    ins["onesd"] = np.ones((128, 128), dtype=np.float32)
    return ins


def kernel(L0_attn_x, L0_attn_qkv_w, L0_attn_qkv_b, L0_attn_out_w, L0_attn_out_b,
           L0_ffn_w1, L0_ffn_conv_w, L0_ffn_conv_b, L0_ffn_w2):
    x = np.asarray(L0_attn_x, dtype=np.float32)
    B = x.shape[0]
    ins = host_inputs(L0_attn_qkv_w, L0_attn_qkv_b, L0_attn_out_w, L0_attn_out_b,
                      L0_ffn_w1, L0_ffn_conv_w, L0_ffn_conv_b, L0_ffn_w2)
    nc = build_program(T=x.shape[1])
    in_maps = []
    for b in range(B):
        m = dict(ins)
        m["x"] = np.ascontiguousarray(x[b].T)
        in_maps.append(m)
    res = run_bass_kernel_spmd(nc, in_maps, core_ids=list(range(B)))
    out = np.stack([np.asarray(r["out"]).T for r in res.results], axis=0)
    return out
